# version 08
# baseline (speedup 1.0000x reference)
import contextlib
import numpy as np
import ml_dtypes
import concourse.bass as bass
import concourse.mybir as mybir
from concourse.bass_utils import run_bass_kernel_spmd

F32 = mybir.dt.float32
BF16 = mybir.dt.bfloat16
I32 = mybir.dt.int32
AF = mybir.ActivationFunctionType
ALU = mybir.AluOpType
AX = mybir.AxisListType
BFD = ml_dtypes.bfloat16

D = 4096
SEQ = 4096
NB = 4
NCORES = 8
NH = 32
NKV = 8
HD = 128
GRP = NH // NKV
II = 11008
NIT = II // 128
NCH = D // 128
TOK = 2048
EPS = 1e-5
SM_SCALE = 1.0 / float(np.sqrt(np.float32(HD)))
SHIFT = -20.0

TAPS = ()
PHASES = ("A1", "A2", "C0", "B", "C1", "C2", "C3")
C0_PARTS = ("zsm", "gu", "down")
C0_DOWN_LVL = 3


class SemC:
    __slots__ = ("name", "h", "n")

    def __init__(self, name):
        self.name = name
        self.h = None
        self.n = 0


class Prog:

    def __init__(self):
        self.evts = []
        self.sems = []
        self.dsems = []

    def sem(self, name):
        s = SemC(name)
        self.sems.append(s)
        return s

    def dsem(self, name):
        s = self.sem(name)
        self.dsems.append(s)
        return s

    def op(self, eng, fn, waits=(), inc=None, inc_k=1):
        w = tuple((s, v) for (s, v) in waits if v > 0)
        self.evts.append((eng, fn, w, inc, inc_k))
        if inc is not None:
            inc.n += inc_k
        return inc.n if inc is not None else None

    def dma(self, eng, out_ap, in_ap, waits=(), sem=None, chain=False):
        w = tuple(waits)
        if not chain and sem.n > 0:
            w = w + ((sem, sem.n),)

        def fn(e, o=out_ap, i=in_ap):
            return e.dma_start(out=o, in_=i)
        return self.op(eng, fn, w, sem, 16)

    def dma_waits(self):
        return tuple((s, s.n) for s in self.dsems if s.n > 0)

    def replay(self, eng_name, eng_obj):
        for (eng, fn, waits, inc, k) in self.evts:
            if eng != eng_name:
                continue
            for (s, v) in waits:
                eng_obj.wait_ge(s.h, v)
            ins = fn(eng_obj)
            if inc is not None:
                ins.then_inc(inc.h, k)


def _rope_tables():
    inv_freq = (1.0 / (10000.0 ** (np.arange(0, HD, 2, dtype=np.float32) / np.float32(HD)))).astype(np.float32)
    t = np.arange(SEQ, dtype=np.float32)
    freqs = np.outer(t, inv_freq).astype(np.float32)
    emb = np.concatenate([freqs, freqs], axis=-1)
    return np.cos(emb).astype(np.float32), np.sin(emb).astype(np.float32)


def _masks_for_parity(p):
    ones = np.ones((128, 128), np.float32)
    diag = np.triu(ones)
    zero = np.zeros((128, 128), np.float32)
    out = np.empty((8, 128, 128), np.float32)
    for i in range(4):
        for tau in range(2):
            d = 3 - i - 2 * tau - p
            out[i * 2 + tau] = ones if d < 0 else (diag if d == 0 else zero)
    return out.astype(BFD)


def build_nc():
    nc = bass.Bass("TRN2", target_bir_lowering=False)
    P = Prog()

    def dram(name, shape, dtype, kind="Internal"):
        if kind == "Internal" and name in TAPS:
            kind = "ExternalOutput"
        return nc.dram_tensor(name, shape, dtype, kind=kind).ap()

    x_batch = dram("x_batch", [SEQ, D], F32, "ExternalInput")
    x_own = dram("x_own", [TOK, D], F32, "ExternalInput")
    wqT = dram("wqT", [D, D], F32, "ExternalInput")
    wkT = dram("wkT", [D, NKV * HD], F32, "ExternalInput")
    wvT = dram("wvT", [D, NKV * HD], F32, "ExternalInput")
    woT = dram("woT", [D, D], F32, "ExternalInput")
    qgN = dram("qgN", [II, D], I32, "ExternalInput")
    quN = dram("quN", [II, D], I32, "ExternalInput")
    qdN = dram("qdN", [D, II], I32, "ExternalInput")
    sgP = dram("sgP", [128, NIT * NCH], F32, "ExternalInput")
    suP = dram("suP", [128, NIT * NCH], F32, "ExternalInput")
    sdP = dram("sdP", [128, NCH * NIT], F32, "ExternalInput")
    sgT = dram("sgT", [NCH, II], F32, "ExternalInput")
    zgT = dram("zgT", [NCH, II], F32, "ExternalInput")
    suT = dram("suT", [NCH, II], F32, "ExternalInput")
    zuT = dram("zuT", [NCH, II], F32, "ExternalInput")
    sdT = dram("sdT", [NIT, D], F32, "ExternalInput")
    zdT = dram("zdT", [NIT, D], F32, "ExternalInput")
    cosK = dram("cosK", [HD, SEQ], F32, "ExternalInput")
    sinK = dram("sinK", [HD, SEQ], F32, "ExternalInput")
    cosQ = dram("cosQ", [HD, TOK], F32, "ExternalInput")
    sinQ = dram("sinQ", [HD, TOK], F32, "ExternalInput")
    rotT = dram("rotT", [HD, HD], BF16, "ExternalInput")
    idb = dram("idb", [128, 128], BF16, "ExternalInput")
    idf = dram("idf", [128, 128], F32, "ExternalInput")
    onesc = dram("onesc", [128, 1], F32, "ExternalInput")
    onescb = dram("onescb", [128, 1], BF16, "ExternalInput")
    onesr = dram("onesr", [1, 128], F32, "ExternalInput")
    onesrb = dram("onesrb", [1, 128], BF16, "ExternalInput")
    qmask = dram("qmask", [8, 128, 128], BF16, "ExternalInput")
    constv = dram("constv", [128, 8], F32, "ExternalInput")

    out_d = dram("out", [TOK, D], F32, "ExternalOutput")

    ln1T_all_d = dram("ln1T_all_d", [NCH, 128, SEQ], BF16)
    KT_d = dram("KT_d", [NKV, 32, 128, 128], BF16)
    V_d = dram("V_d", [NKV, 32, 128, 128], BF16)
    QT_d = dram("QT_d", [NH * HD, TOK], BF16)
    x1T_d = dram("x1T_d", [NCH, 128, TOK], F32)
    ln2g_d = dram("ln2g_d", [NCH, TOK], BF16)
    hg_d = dram("hg_d", [NIT, TOK], BF16)
    hT_d = dram("hT_d", [NIT, 128, TOK], BF16)
    WgB_d = dram("WgB_d", [NIT, 128, NCH, 128], BF16)
    WuB_d = dram("WuB_d", [NIT, 128, NCH, 128], BF16)
    WdB_d = dram("WdB_d", [NCH, 128, NIT, 128], BF16)
    zsmg_d = dram("zsmg_d", [2 * NCH, II // 2], BF16)
    zsmu_d = dram("zsmu_d", [2 * NCH, II // 2], BF16)
    zsmd_d = dram("zsmd_d", [NIT, D], BF16)
    attnT_dbg = dram("attnT_dbg", [NH * HD, TOK], BF16) if "attnT_dbg" in TAPS else None

    tT = P.sem("tT")
    vV = P.sem("vV")
    sS = P.sem("sS")
    gG = P.sem("gG")

    ctx = contextlib.ExitStack()

    rot_sb = ctx.enter_context(nc.sbuf_tensor("rot_sb", [HD, HD], BF16))
    idb_sb = ctx.enter_context(nc.sbuf_tensor("idb_sb", [128, 128], BF16))
    idf_sb = ctx.enter_context(nc.sbuf_tensor("idf_sb", [128, 128], F32))
    onesc_sb = ctx.enter_context(nc.sbuf_tensor("onesc_sb", [128, 1], F32))
    onescb_sb = ctx.enter_context(nc.sbuf_tensor("onescb_sb", [128, 1], BF16))
    onesr_sb = ctx.enter_context(nc.sbuf_tensor("onesr_sb", [1, 128], F32))
    onesrb_sb = ctx.enter_context(nc.sbuf_tensor("onesrb_sb", [1, 128], BF16))
    qm_sb = ctx.enter_context(nc.sbuf_tensor("qm_sb", [128, 8 * 128], BF16))
    cv_sb = ctx.enter_context(nc.sbuf_tensor("cv_sb", [128, 8], F32))
    nc.const_aps.register(F32, EPS, cv_sb[:, 0:1])
    nc.const_aps.register(F32, SHIFT, cv_sb[:, 1:2])
    cst = P.dsem("cst")
    P.dma("sync", cv_sb[:], constv, sem=cst)
    for dst, src in ((rot_sb[:], rotT), (idb_sb[:], idb), (idf_sb[:], idf),
                     (onesc_sb[:], onesc), (onescb_sb[:], onescb),
                     (onesr_sb[:], onesr), (onesrb_sb[:], onesrb),
                     (qm_sb[:].rearrange("p (m f) -> p m f", m=8), qmask.transpose([1, 0, 2]))):
        P.dma("sync", dst, src, sem=cst, chain=True)
    cst_n = cst.n

    st = {}

    def ln_tile(xb_ap, ln_out_ap, dump, smb, slot, waits_x, war_norm=0):
        sm = smb[:, slot * 8:(slot + 1) * 8]
        rs_v = P.op("vector", lambda e, o=sm[:, 0:1], i=xb_ap: e.reduce_sum(out=o, in_=i, axis=AX.X),
                    waits_x, vV)
        sqs = P.op("scalar", lambda e, o=dump[:], i=xb_ap, a=sm[:, 1:2]: e.activation(o, i, AF.Square, accum_out=a),
                   waits_x, sS)
        m_v = P.op("vector", lambda e, o=sm[:, 2:3], i=sm[:, 0:1]: e.tensor_scalar(o, i, 1.0 / D, None, ALU.mult),
                   ((vV, rs_v),), vV)
        msq_v = P.op("vector", lambda e, o=sm[:, 3:4], i=sm[:, 1:2]: e.tensor_scalar(o, i, 1.0 / D, None, ALU.mult),
                     ((sS, sqs),), vV)
        m2_v = P.op("vector", lambda e, o=sm[:, 4:5], a=sm[:, 2:3]: e.tensor_tensor(o, a, a, ALU.mult),
                    ((vV, m_v),), vV)
        var_v = P.op("vector", lambda e, o=sm[:, 4:5], a=sm[:, 3:4], b=sm[:, 4:5]: e.tensor_tensor(o, a, b, ALU.subtract),
                     ((vV, m2_v), (vV, msq_v)), vV)
        ln_s = P.op("scalar", lambda e, o=sm[:, 5:6], i=sm[:, 4:5]: e.activation(o, i, AF.Ln, bias=EPS),
                    ((vV, var_v),), sS)
        r_s = P.op("scalar", lambda e, o=sm[:, 6:7], i=sm[:, 5:6]: e.activation(o, i, AF.Exp, scale=-0.5),
                   ((sS, ln_s),), sS)
        mr_v = P.op("vector",
                    lambda e, o=sm[:, 7:8], i=sm[:, 2:3], r=sm[:, 6:7]: e.scalar_tensor_tensor(o, i, -1.0, r, ALU.mult, ALU.mult),
                    ((sS, r_s),), vV)
        P.op("vector",
             lambda e, o=ln_out_ap, i=xb_ap, r=sm[:, 6:7], m=sm[:, 7:8]: e.tensor_scalar(o, i, r, m, ALU.mult, ALU.add),
             ((tT, war_norm), (vV, mr_v)), vV)
        return vV.n

    def transposes_128(src_ap_fn, dst_psum_fn, n, first_waits, id_sb):
        for c in range(n):
            P.op("tensor",
                 lambda e, o=dst_psum_fn(c), i=src_ap_fn(c): e.transpose(o, i, id_sb[:]),
                 first_waits if c == 0 else (), tT if c == n - 1 else None)
        return tT.n

    def rope_block(kbf, pR, t1, t2, out_ap, cos_slice, sin_slice, evac_s, war_out, tab_w):
        P.op("tensor", lambda e, o=pR[:], l=rot_sb[:], r=kbf[:]: e.matmul(o, l, r, start=True, stop=True),
             ((sS, evac_s), (vV, st.get("rope_t2", 0))), tT)
        rot_t = tT.n
        t1_v = P.op("vector", lambda e, o=t1[:], i=kbf[:], cc=cos_slice: e.tensor_tensor(out=o, in0=i, in1=cc, op=ALU.mult),
                    ((sS, evac_s), tab_w, (vV, st.get("rope_add", 0))), vV)
        st["rope_t1"] = t1_v
        t2_v = P.op("vector", lambda e, o=t2[:], i=pR[:], cc=sin_slice: e.tensor_tensor(out=o, in0=i, in1=cc, op=ALU.mult),
                    ((tT, rot_t), (vV, st.get("rope_add", 0))), vV)
        st["rope_t2"] = t2_v
        add_v = P.op("vector", lambda e, o=out_ap, a_=t1[:], b=t2[:]: e.tensor_tensor(out=o, in0=a_, in1=b, op=ALU.add),
                     (war_out, (vV, t1_v), (vV, t2_v)), vV)
        st["rope_add"] = add_v
        return add_v

    if "A1" in PHASES:
        with contextlib.ExitStack() as actx:
            sba = lambda n, s, d: actx.enter_context(nc.sbuf_tensor(n, s, d))
            psa = lambda n, s, d: actx.enter_context(nc.psum_tensor(n, s, d))
            wst = sba("a_wst", [128, 1024], F32)
            wkv_bf = sba("a_wkv", [128, NCH * 1024], BF16)
            xb = sba("a_xb", [128, 2 * D], F32)
            smb = sba("a_smb", [128, 32], F32)
            lnb = sba("a_lnb", [128, 2 * D], BF16)
            gbuf = sba("a_gbuf", [128, NCH * 512], BF16)
            kbf = sba("a_kbf", [128, 2 * 512], BF16)
            t1 = sba("a_t1", [128, 512], F32)
            t2 = sba("a_t2", [128, 512], F32)
            krb = sba("a_krb", [128, 2 * 512], BF16)
            csK = sba("a_csK", [128, SEQ], F32)
            snK = sba("a_snK", [128, SEQ], F32)
            pT = psa("a_pT", [128, D], BF16)
            pK = psa("a_pK", [128, 2 * 512], F32)
            pR = psa("a_pR", [128, 2 * 512], F32)

            a_cs = P.dsem("a_cs")
            a_x = [P.dsem("a_x0"), P.dsem("a_x1")]
            a_w = P.dsem("a_w")
            a_gld = P.dsem("a_gld")
            a_gsp = P.dsem("a_gsp")
            a_kr = [P.dsem("a_kr0"), P.dsem("a_kr1")]
            P.dma("sync", csK[:], cosK, waits=((cst, cst_n),), sem=a_cs)
            P.dma("sync", snK[:], sinK, sem=a_cs, chain=True)
            cs_n = a_cs.n

            def load_wkv(src_dram):
                t0 = tT.n
                for c in range(NCH):
                    ld = P.dma("sync", wst[:], src_dram[c * 128:(c + 1) * 128, :],
                               waits=((vV, st.get("wcast", 0)),), sem=a_w)
                    P.op("vector",
                         lambda e, o=wkv_bf[:, c * 1024:(c + 1) * 1024], i=wst[:]: e.tensor_copy(out=o, in_=i),
                         ((a_w, ld), (tT, t0)), vV)
                    st["wcast"] = vV.n

            def a1_pass(which, w_ready_v):
                for a in range(8):
                    if which == "K":
                        for tau in range(4):
                            tglob = 4 * a + tau
                            slot = tau % 2
                            buf = xb[:, slot * D:slot * D + D]
                            ld = P.dma("sync", buf, x_batch[tglob * 128:(tglob + 1) * 128, :],
                                       waits=((vV, st.get(f"a_xfree{slot}", 0)),), sem=a_x[slot])
                            lslot = lnb[:, slot * D:(slot + 1) * D]
                            vln = ln_tile(buf, lslot, lslot, smb, tau, ((a_x[slot], ld), (cst, cst_n)),
                                          war_norm=st.get(f"a_lnb_tpose{slot}", 0))
                            t_aft = transposes_128(
                                lambda c: lslot[:, c * 128:(c + 1) * 128],
                                lambda c: pT[:, c * 128:(c + 1) * 128],
                                NCH, ((vV, vln), (vV, st.get("a_pT_evac", 0))), idb_sb)
                            st[f"a_lnb_tpose{slot}"] = t_aft
                            dst = gbuf[:].rearrange("p (c q) -> p c q", c=NCH)[:, :, tau * 128:(tau + 1) * 128]
                            src = pT[:].rearrange("p (c q) -> p c q", c=NCH)
                            P.op("vector", lambda e, o=dst, i=src: e.tensor_copy(out=o, in_=i),
                                 ((tT, t_aft), (a_gsp, st.get("a_gspill", 0))), vV)
                            st["a_pT_evac"] = vV.n
                            st[f"a_xfree{slot}"] = vV.n
                            st["a_gbuf_v"] = vV.n
                        P.dma("scalar",
                              ln1T_all_d[:, :, a * 512:(a + 1) * 512].transpose([1, 0, 2]),
                              gbuf[:].rearrange("p (c q) -> p c q", c=NCH),
                              waits=((vV, st["a_gbuf_v"]),), sem=a_gsp)
                        st["a_gspill"] = a_gsp.n
                        mm_w0 = ((vV, st["a_gbuf_v"]),)
                    else:
                        ld = P.dma("sync", gbuf[:].rearrange("p (c q) -> p c q", c=NCH),
                                   ln1T_all_d[:, :, a * 512:(a + 1) * 512].transpose([1, 0, 2]),
                                   waits=((tT, tT.n), (a_gsp, a_gsp.n)), sem=a_gld)
                        mm_w0 = ((a_gld, ld), (vV, w_ready_v))

                    def emit_rope_k(f, evac_s):
                        bank = f % 2
                        kslot = kbf[:, bank * 512:(bank + 1) * 512]
                        dst_slot = krb[:, bank * 512:bank * 512 + 512]
                        kv_v = rope_block(kslot, pR[:, bank * 512:(bank + 1) * 512], t1, t2, dst_slot,
                                          csK[:, st["a_cur"] * 512:(st["a_cur"] + 1) * 512],
                                          snK[:, st["a_cur"] * 512:(st["a_cur"] + 1) * 512],
                                          evac_s, (a_kr[bank], st.get(f"a_krbq{bank}", 0)),
                                          (a_cs, cs_n))
                        st[f"rope_t1b{bank}"] = st["rope_t1"]
                        P.dma("scalar",
                              KT_d[f, 4 * st["a_cur"]:4 * st["a_cur"] + 4].transpose([1, 0, 2]),
                              dst_slot.rearrange("p (b q) -> p b q", b=4),
                              waits=((vV, kv_v),), sem=a_kr[bank])
                        st[f"a_krbq{bank}"] = a_kr[bank].n

                    pending_rope = st.pop("a_pend", None)
                    st["a_cur_next"] = a
                    for f in range(NKV):
                        bank = f % 2
                        pKb = pK[:, bank * 512:(bank + 1) * 512]
                        w0 = mm_w0 + ((sS, st.get(f"a_evac{bank}", 0)),)
                        for c in range(NCH):
                            P.op("tensor",
                                 lambda e, o=pKb, l=wkv_bf[:, (c * 8 + f) * 128:(c * 8 + f + 1) * 128],
                                 r=gbuf[:, c * 512:(c + 1) * 512], s=(c == 0), t=(c == NCH - 1):
                                 e.matmul(o, l, r, start=s, stop=t),
                                 w0 if c == 0 else (), tT if c == NCH - 1 else None)
                        mm_done = tT.n
                        kslot = kbf[:, bank * 512:(bank + 1) * 512]
                        P.op("scalar", lambda e, o=kslot, i=pKb: e.copy(out=o, in_=i),
                             ((tT, mm_done), (vV, st.get(f"rope_t1b{bank}", 0))), sS)
                        st[f"a_evac{bank}"] = sS.n
                        evac_s = sS.n
                        if which == "K":
                            if pending_rope is not None:
                                st["a_cur"] = pending_rope[2]
                                emit_rope_k(pending_rope[0], pending_rope[1])
                            pending_rope = (f, evac_s, a)
                        else:
                            slot = f % 2
                            dst_slot = krb[:, slot * 512:slot * 512 + 512]
                            dummy = True
                            tr_t = transposes_128(
                                lambda b_: kslot[:, b_ * 128:(b_ + 1) * 128],
                                lambda b_: pT[:, b_ * 128:(b_ + 1) * 128],
                                4, ((sS, evac_s), (vV, st.get("a_pTv_evac", 0))), idb_sb)
                            st[f"rope_t1b{bank}"] = 0
                            P.op("vector", lambda e, o=dst_slot, i=pT[:, 0:512]: e.tensor_copy(out=o, in_=i),
                                 ((tT, tr_t), (a_kr[slot], st.get(f"a_krbq{slot}", 0))), vV)
                            st["a_pTv_evac"] = vV.n
                            kv_v = vV.n
                            P.dma("scalar",
                                  V_d[f, 4 * a:4 * a + 4].transpose([1, 0, 2]),
                                  dst_slot.rearrange("p (b q) -> p b q", b=4),
                                  waits=((vV, kv_v),), sem=a_kr[slot])
                            st[f"a_krbq{slot}"] = a_kr[slot].n
                    if which == "K":
                        st["a_pend"] = pending_rope
                        if a == 7 and pending_rope is not None:
                            st["a_cur"] = pending_rope[2]
                            emit_rope_k(pending_rope[0], pending_rope[1])
                            st.pop("a_pend")

            load_wkv(wkT)
            a1_pass("K", st["wcast"])
            load_wkv(wvT)
            a1_pass("V", st["wcast"])

    if "A2" in PHASES:
        with contextlib.ExitStack() as actx:
            sba = lambda n, s, d: actx.enter_context(nc.sbuf_tensor(n, s, d))
            psa = lambda n, s, d: actx.enter_context(nc.psum_tensor(n, s, d))
            lnT = sba("q_lnT", [128, NCH * TOK], BF16)
            csQ = sba("q_csQ", [128, TOK], F32)
            snQ = sba("q_snQ", [128, TOK], F32)
            pT = psa("q_pT", [128, D], BF16)
            pQ = psa("q_pQ", [128, 2 * 512], F32)
            pR = psa("q_pR", [128, 512], F32)

            q_cs = P.dsem("q_cs")
            q_x = [P.dsem("q_x0"), P.dsem("q_x1")]
            q_w = [P.dsem("q_w0"), P.dsem("q_w1")]
            q_qr = [P.dsem("q_qr0"), P.dsem("q_qr1")]
            P.dma("sync", csQ[:], cosQ,
                  waits=((vV, vV.n), (sS, sS.n), (tT, tT.n)) + P.dma_waits(), sem=q_cs)
            P.dma("sync", snQ[:], sinQ, sem=q_cs, chain=True)
            qcs_n = q_cs.n

            with contextlib.ExitStack() as bctx:
                sbb = lambda n, s, d: bctx.enter_context(nc.sbuf_tensor(n, s, d))
                xb = sbb("q_xb", [128, 2 * D], F32)
                smb = sbb("q_smb", [128, 32], F32)
                lnb = sbb("q_lnb", [128, 2 * D], BF16)
                for u in range(4):
                    for tau in range(4):
                        j = 4 * u + tau
                        slot = tau % 2
                        buf = xb[:, slot * D:slot * D + D]
                        ld = P.dma("sync", buf, x_own[j * 128:(j + 1) * 128, :],
                                   waits=((vV, st.get(f"q_xfree{slot}", 0)),), sem=q_x[slot])
                        lslot = lnb[:, slot * D:(slot + 1) * D]
                        vln = ln_tile(buf, lslot, lslot, smb, tau, ((q_x[slot], ld),),
                                      war_norm=st.get(f"q_lnb_tpose{slot}", 0))
                        t_aft = transposes_128(
                            lambda c: lslot[:, c * 128:(c + 1) * 128],
                            lambda c: pT[:, c * 128:(c + 1) * 128],
                            NCH, ((vV, vln), (vV, st.get("q_pT_evac", 0))), idb_sb)
                        st[f"q_lnb_tpose{slot}"] = t_aft
                        dst = lnT[:].rearrange("p (c q) -> p c q", c=NCH)[:, :, u * 512 + tau * 128:u * 512 + (tau + 1) * 128]
                        src = pT[:].rearrange("p (c q) -> p c q", c=NCH)
                        P.op("vector", lambda e, o=dst, i=src: e.tensor_copy(out=o, in_=i),
                             ((tT, t_aft),), vV)
                        st["q_pT_evac"] = vV.n
                        st[f"q_xfree{slot}"] = vV.n
            st["q_lnT_v"] = vV.n

            wqs = sba("q_wqs", [128, 2 * NCH * 128], F32)
            wqb = sba("q_wqb", [128, 2 * NCH * 128], BF16)
            kbf = sba("q_kbf", [128, 2 * 512], BF16)
            t1 = sba("q_t1", [128, 512], F32)
            t2 = sba("q_t2", [128, 512], F32)
            qrb = sba("q_qrb", [128, 2 * 512], BF16)

            for f in range(NH):
                wslot = f % 2
                wsts = wqs[:, wslot * D:wslot * D + D]
                wqbs = wqb[:, wslot * D:wslot * D + D]
                for part in range(4):
                    P.dma("sync",
                          wsts.rearrange("p (c f) -> p c f", c=NCH)[:, part * 8:(part + 1) * 8, :],
                          wqT[:, f * 128:(f + 1) * 128].rearrange("(c p) f -> p c f", p=128)[:, part * 8:(part + 1) * 8, :],
                          waits=((vV, max(st.get(f"q_wcast{wslot}", 0), st["q_lnT_v"])),) if part == 0 else (),
                          sem=q_w[wslot], chain=(part > 0))
                wq_ld = q_w[wslot].n
                P.op("vector", lambda e, o=wqbs, i=wsts: e.tensor_copy(out=o, in_=i),
                     ((q_w[wslot], wq_ld), (tT, st.get(f"q_wmm{wslot}", 0))), vV)
                st[f"q_wcast{wslot}"] = vV.n
                wcast = vV.n
                for u in range(4):
                    bank = u % 2
                    pQb = pQ[:, bank * 512:(bank + 1) * 512]
                    w0 = ((vV, wcast), (sS, st.get(f"q_evac{bank}", 0)))
                    for c in range(NCH):
                        P.op("tensor",
                             lambda e, o=pQb, l=wqbs[:, c * 128:(c + 1) * 128],
                             r=lnT[:, c * TOK + u * 512:c * TOK + (u + 1) * 512],
                             s=(c == 0), t=(c == NCH - 1):
                             e.matmul(o, l, r, start=s, stop=t),
                             w0 if c == 0 else (), tT if c == NCH - 1 else None)
                    st[f"q_wmm{wslot}"] = tT.n
                    mm_done = tT.n
                    kslot = kbf[:, bank * 512:(bank + 1) * 512]
                    P.op("scalar", lambda e, o=kslot, i=pQb: e.copy(out=o, in_=i),
                         ((tT, mm_done), (vV, st.get(f"rope_t1b{bank}", 0))), sS)
                    st[f"q_evac{bank}"] = sS.n
                    evac_s = sS.n
                    slot = u % 2
                    dst_slot = qrb[:, slot * 512:slot * 512 + 512]
                    kv_v = rope_block(kslot, pR, t1, t2, dst_slot,
                                      csQ[:, u * 512:(u + 1) * 512],
                                      snQ[:, u * 512:(u + 1) * 512],
                                      evac_s, (q_qr[slot], st.get(f"q_qrbq{slot}", 0)),
                                      (q_cs, qcs_n))
                    st[f"rope_t1b{bank}"] = st["rope_t1"]
                    P.dma("scalar", QT_d[f * 128:(f + 1) * 128, u * 512:(u + 1) * 512], dst_slot,
                          waits=((vV, kv_v),), sem=q_qr[slot])
                    st[f"q_qrbq{slot}"] = q_qr[slot].n

    def guard():
        return ((vV, vV.n), (sS, sS.n), (tT, tT.n), (gG, gG.n)) + P.dma_waits()

    if "C0" in PHASES:
        with contextlib.ExitStack() as actx:
            sba = lambda n, s, d: actx.enter_context(nc.sbuf_tensor(n, s, d))
            psa = lambda n, s, d: actx.enter_context(nc.psum_tensor(n, s, d))
            pT0 = psa("c0_pT0", [128, D], BF16)
            c_sz = P.dsem("c_sz")
            c_zsp = P.dsem("c_zsp")
            c_qb = [P.dsem("c_qb0"), P.dsem("c_qb1")]
            c_ws = [P.dsem("c_ws0"), P.dsem("c_ws1")]

            with contextlib.ExitStack() as bctx:
              if "zsm" in C0_PARTS:
                  sbb = lambda n, s, d: bctx.enter_context(nc.sbuf_tensor(n, s, d))
                  s_st = sbb("c0_s", [NCH, II], F32)
                  z_st = sbb("c0_z", [NCH, II], F32)
                  zs_sb = sbb("c0_zs", [NCH, II], BF16)
                  sd_st = sbb("c0_sd", [NIT, D], F32)
                  zd_st = sbb("c0_zd", [NIT, D], F32)
                  zsd_sb = sbb("c0_zsd", [NIT, D], BF16)
                  first = True
                  for (s_dram, z_dram, zdst) in ((sgT, zgT, zsmg_d), (suT, zuT, zsmu_d)):
                      w0 = guard() if first else ((c_zsp, c_zsp.n),)
                      first = False
                      P.dma("sync", s_st[:], s_dram, waits=w0, sem=c_sz)
                      l2 = P.dma("sync", z_st[:], z_dram, sem=c_sz, chain=True)
                      vz = []
                      for par in range(2):
                          src_z = z_st[:].rearrange("g (j two i) -> g j two i", two=2, i=128)[:, :, par, :]
                          src_s = s_st[:].rearrange("g (j two i) -> g j two i", two=2, i=128)[:, :, par, :]
                          o = zs_sb[:, par * (II // 2):(par + 1) * (II // 2)].rearrange("g (j i) -> g j i", i=128)
                          vz.append(P.op("vector",
                                         lambda e, o=o, a_=src_z, b=src_s: e.scalar_tensor_tensor(o, a_, -1.0, b, ALU.mult, ALU.mult),
                                         ((c_sz, l2),), vV))
                      P.dma("scalar", zdst.rearrange("(two g) i -> g two i", g=NCH),
                            zs_sb[:].rearrange("g (two i) -> g two i", two=2),
                            waits=((vV, vz[-1]),), sem=c_zsp)
                  P.dma("sync", sd_st[:], sdT, waits=((c_zsp, c_zsp.n),), sem=c_sz)
                  l2 = P.dma("sync", zd_st[:], zdT, sem=c_sz, chain=True)
                  vz = P.op("vector", lambda e, o=zsd_sb[:], a_=zd_st[:], b=sd_st[:]:
                            e.scalar_tensor_tensor(o, a_, -1.0, b, ALU.mult, ALU.mult),
                            ((c_sz, l2),), vV)
                  P.dma("scalar", zsmd_d, zsd_sb[:], waits=((vV, vz),), sem=c_zsp)

            with contextlib.ExitStack() as bctx:
              if "gu" in C0_PARTS:
                  sbb = lambda n, s, d: bctx.enter_context(nc.sbuf_tensor(n, s, d))
                  sgp_sb = sbb("c0_sgp", [128, NIT * NCH], F32)
                  sup_sb = sbb("c0_sup", [128, NIT * NCH], F32)
                  qblk = sbb("c0_qblk", [128, 2 * D], I32)
                  tsb = sbb("c0_tsb", [128, 2 * D], BF16)
                  wvout = sbb("c0_wv", [128, 2 * D], BF16)
                  P.dma("sync", sgp_sb[:], sgP, waits=((c_zsp, c_zsp.n),), sem=c_sz)
                  spl = P.dma("sync", sup_sb[:], suP, sem=c_sz, chain=True)
                  for (q_dram, sp_sb, wdst, mtag) in ((qgN, sgp_sb, WgB_d, "g"), (quN, sup_sb, WuB_d, "u")):
                      for it in range(NIT):
                          slot = it % 2
                          qb = qblk[:, slot * D:(slot + 1) * D]
                          tb = tsb[:, slot * D:(slot + 1) * D]
                          wo = wvout[:, slot * D:(slot + 1) * D]
                          qld = P.dma("sync", qb, q_dram[it * 128:(it + 1) * 128, :],
                                      waits=((vV, st.get(f"c0_ts{slot}", 0)), (gG, st.get(f"c0_tsg{slot}", 0))),
                                      sem=c_qb[slot])
                          eng, sem = ("vector", vV)
                          bc = sp_sb[:, it * NCH:(it + 1) * NCH].rearrange("p (c o) -> p c o", o=1).broadcast_to([128, NCH, 128])
                          P.op(eng,
                               lambda e, o=tb.rearrange("p (c i) -> p c i", i=128),
                               i=qb.rearrange("p (c i) -> p c i", i=128), b=bc:
                               e.tensor_tensor(out=o, in0=i, in1=b, op=ALU.mult),
                               ((c_qb[slot], qld), (c_sz, spl),
                                (tT, st.get(f"c0_tp{slot}", 0))), sem)
                          ts_v, ts_g = vV.n, gG.n
                          st[f"c0_ts{slot}"] = ts_v
                          st[f"c0_tsg{slot}"] = ts_g
                          tp = transposes_128(
                              lambda c: tb[:, c * 128:(c + 1) * 128],
                              lambda c: pT0[:, c * 128:(c + 1) * 128],
                              NCH, ((vV, ts_v), (gG, ts_g), (vV, st.get("c0_ev", 0)), (sS, st.get("c0_evs", 0))), idb_sb)
                          st[f"c0_tp{slot}"] = tp
                          half = NCH // 2
                          ev_v = P.op("vector",
                                      lambda e, o=wo[:, 0:half * 128], i=pT0[:, 0:half * 128]:
                                      e.tensor_copy(out=o, in_=i),
                                      ((tT, tp), (c_ws[slot], st.get(f"c0_wsp{slot}", 0))), vV)
                          st["c0_ev"] = ev_v
                          ev_s = P.op("scalar",
                                      lambda e, o=wo[:, half * 128:], i=pT0[:, half * 128:]:
                                      e.copy(out=o, in_=i),
                                      ((tT, tp), (c_ws[slot], st.get(f"c0_wsp{slot}", 0))), sS)
                          st["c0_evs"] = ev_s
                          P.dma("scalar", wdst[it], wo.rearrange("p (c i) -> p c i", i=128),
                                waits=((vV, ev_v), (sS, ev_s)), sem=c_ws[slot])
                          st[f"c0_wsp{slot}"] = c_ws[slot].n

            with contextlib.ExitStack() as bctx:
              if "down" in C0_PARTS:
                  sbb = lambda n, s, d: bctx.enter_context(nc.sbuf_tensor(n, s, d))
                  sdp_sb = sbb("c0_sdp", [128, NCH * NIT], F32)
                  qcol = sbb("c0_qcol", [128, 2 * II], I32)
                  tsb2 = sbb("c0_tsb2", [128, 2 * II], BF16)
                  wdout = sbb("c0_wd", [128, 2 * II], BF16)
                  sdl = P.dma("sync", sdp_sb[:], sdP, waits=((c_ws[0], c_ws[0].n), (c_ws[1], c_ws[1].n)), sem=c_sz)
                  for ft in range(NCH):
                      slot = ft % 2
                      qb = qcol[:, slot * II:(slot + 1) * II]
                      tb = tsb2[:, slot * II:(slot + 1) * II]
                      wo = wdout[:, slot * II:(slot + 1) * II]
                      qld = P.dma("sync", qb, qdN[ft * 128:(ft + 1) * 128, :],
                                  waits=((vV, st.get(f"c0_dts{slot}", 0)), (gG, st.get(f"c0_dtsg{slot}", 0))),
                                  sem=c_qb[slot])
                      for ch0 in range(0, NIT, 32):
                          nb = min(32, NIT - ch0)
                          bc = sdp_sb[:, ft * NIT + ch0:ft * NIT + ch0 + nb].rearrange("p (c o) -> p c o", o=1).broadcast_to([128, nb, 128])
                          P.op("vector",
                               lambda e, o=tb[:, ch0 * 128:(ch0 + nb) * 128].rearrange("p (c i) -> p c i", i=128),
                               i=qb[:, ch0 * 128:(ch0 + nb) * 128].rearrange("p (c i) -> p c i", i=128), b=bc:
                               e.tensor_tensor(out=o, in0=i, in1=b, op=ALU.mult),
                               ((c_qb[slot], qld), (c_sz, sdl),
                                (tT, st.get(f"c0_dtp{slot}", 0))) if ch0 == 0 else (), vV)
                      st[f"c0_dts{slot}"] = vV.n
                      st[f"c0_dtsg{slot}"] = gG.n
                      ts_v, ts_g = vV.n, gG.n
                      if C0_DOWN_LVL < 2:
                          st[f"c0_dtp{slot}"] = tT.n
                          st[f"c0_dsp{slot}"] = c_ws[slot].n
                          continue
                      for ch0 in range(0, NIT, 32):
                          nblk = min(32, NIT - ch0)
                          tp = transposes_128(
                              lambda b, base=ch0: tb[:, (base + b) * 128:(base + b + 1) * 128],
                              lambda b: pT0[:, b * 128:(b + 1) * 128],
                              nblk, ((vV, ts_v), (gG, ts_g), (vV, st.get("c0_ev", 0)), (sS, st.get("c0_evs", 0))), idb_sb)
                          st[f"c0_dtp{slot}"] = tp
                          hb_ = min(16, nblk)
                          ev_v = P.op("vector",
                                      lambda e, o=wo[:, ch0 * 128:(ch0 + hb_) * 128], i=pT0[:, 0:hb_ * 128]:
                                      e.tensor_copy(out=o, in_=i),
                                      ((tT, tp), (c_ws[slot], st.get(f"c0_dsp{slot}", 0))), vV)
                          st["c0_ev"] = ev_v
                          ev_s = P.op("scalar",
                                      lambda e, o=wo[:, (ch0 + hb_) * 128:(ch0 + nblk) * 128],
                                      i=pT0[:, hb_ * 128:nblk * 128]:
                                      e.copy(out=o, in_=i),
                                      ((tT, tp), (c_ws[slot], st.get(f"c0_dsp{slot}", 0))), sS)
                          st["c0_evs"] = ev_s
                      if C0_DOWN_LVL >= 3:
                          P.dma("scalar", WdB_d[ft], wo.rearrange("p (it i) -> p it i", i=128),
                                waits=((vV, st["c0_ev"]), (sS, st["c0_evs"])), sem=c_ws[slot])
                      st[f"c0_dsp{slot}"] = c_ws[slot].n

    attnT_scope = contextlib.ExitStack()
    if "B" in PHASES:
        attnT = attnT_scope.enter_context(nc.sbuf_tensor("b_attnT", [128, NCH * TOK], BF16))
        with contextlib.ExitStack() as actx:
            sba = lambda n, s, d: actx.enter_context(nc.sbuf_tensor(n, s, d))
            psa = lambda n, s, d: actx.enter_context(nc.psum_tensor(n, s, d))
            qbuf = sba("b_qbuf", [128, 2 * 1024], BF16)
            kbuf = sba("b_kbuf", [128, 2 * 32 * 128], BF16)
            vbuf = sba("b_vbuf", [128, 2 * 32 * 128], BF16)
            pbv = sba("b_pb", [128, 2 * 1024], BF16)
            avf = sba("b_avf", [128, 1024], F32)
            rrow = sba("b_rrow", [1, 1024], F32)
            pl = psa("b_pl", [128, 2 * 1024], F32)
            pav = psa("b_pav", [128, 1024], F32)
            prb = psa("b_prb", [128, 1024], F32)

            b_q = [P.dsem("b_q0"), P.dsem("b_q1")]
            b_k = [P.dsem("b_k0"), P.dsem("b_k1")]
            b_v = [P.dsem("b_v0"), P.dsem("b_v1")]
            first_b = True
            for gp in range(8):
                E = 4 * gp + 4
                for kv in range(NKV):
                    it_par = (gp * NKV + kv) % 2
                    qb = qbuf[:, it_par * 1024:(it_par + 1) * 1024]
                    kb = kbuf[:, it_par * 4096:(it_par + 1) * 4096]
                    vb = vbuf[:, it_par * 4096:(it_par + 1) * 4096]
                    slot_guard = (tT, st.get(f"b_iter{it_par}", 0))
                    for h in range(GRP):
                        P.dma("sync", qb[:, h * 256:(h + 1) * 256],
                              QT_d[(kv * GRP + h) * 128:(kv * GRP + h + 1) * 128, gp * 256:(gp + 1) * 256],
                              waits=(guard() if first_b else (slot_guard,)) if h == 0 else (),
                              sem=b_q[it_par], chain=(h > 0))
                        first_b = False
                    q_ld = b_q[it_par].n
                    for c in range(E):
                        P.dma("sync", kb[:, c * 128:(c + 1) * 128], KT_d[kv, c],
                              sem=b_k[it_par], chain=(c > 0))
                        P.dma("sync", vb[:, c * 128:(c + 1) * 128], V_d[kv, c],
                              sem=b_v[it_par], chain=(c > 0))
                    k_ld, v_ld = b_k[it_par].n, b_v[it_par].n

                    def emit_av(c, dep):
                        psl = pbv[:, (c % 2) * 1024:(c % 2 + 1) * 1024]
                        w_av = (dep, (b_v[it_par], v_ld)) + (((sS, st.get("b_avevac", 0)),) if c == 0 else ())
                        for half in range(2):
                            P.op("tensor",
                                 lambda e, o=pav[:, half * 512:(half + 1) * 512], l=vb[:, c * 128:(c + 1) * 128],
                                 r=psl[:, half * 512:(half + 1) * 512], s=(c == 0), t=(c == E - 1):
                                 e.matmul(o, l, r, start=s, stop=t),
                                 w_av if half == 0 else (), None)
                        for half in range(2):
                            P.op("tensor",
                                 lambda e, o=prb[0:1, half * 512:(half + 1) * 512],
                                 r=psl[:, half * 512:(half + 1) * 512], s=(c == 0), t=(c == E - 1):
                                 e.matmul(o, onescb_sb[:], r, start=s, stop=t),
                                 ((vV, st.get("b_norm", 0)),) if (c == 0 and half == 0) else (),
                                 tT if (c == E - 1 and half == 1) else None)

                    prev = None
                    for c in range(E):
                        lsl = pl[:, (c % 2) * 1024:(c % 2 + 1) * 1024]
                        psl = pbv[:, (c % 2) * 1024:(c % 2 + 1) * 1024]
                        w0 = ((b_k[it_par], k_ld), (b_q[it_par], q_ld), (sS, st.get(f"b_exp{c % 2}", 0)),
                              (vV, st.get(f"b_msk{c % 2}", 0)))
                        for h in range(GRP):
                            P.op("tensor",
                                 lambda e, o=lsl[:, h * 256:(h + 1) * 256], l=kb[:, c * 128:(c + 1) * 128],
                                 r=qb[:, h * 256:(h + 1) * 256]:
                                 e.matmul(o, l, r, start=True, stop=True),
                                 w0 if h == 0 else (), tT if h == GRP - 1 else None)
                        qk_t = tT.n
                        if prev is not None:
                            emit_av(*prev)
                        ex_s = P.op("scalar",
                                    lambda e, o=psl, i=lsl: e.activation(o, i, AF.Exp, bias=SHIFT, scale=SM_SCALE),
                                    ((tT, qk_t), (vV, st.get(f"b_msk{c % 2}", 0))), sS)
                        st[f"b_exp{c % 2}"] = ex_s
                        dep = (sS, ex_s)
                        if c >= E - 4:
                            i_ = E - 1 - c
                            mv = 0
                            for tau in range(2):
                                for h in range(GRP):
                                    mv = P.op("vector",
                                              lambda e, o=psl[:, h * 256 + tau * 128:h * 256 + (tau + 1) * 128],
                                              m=qm_sb[:, (i_ * 2 + tau) * 128:(i_ * 2 + tau + 1) * 128]:
                                              e.tensor_tensor(out=o, in0=o, in1=m, op=ALU.mult),
                                              ((sS, ex_s),) if (tau == 0 and h == 0) else (), vV)
                            st[f"b_msk{c % 2}"] = mv
                            dep = (vV, mv)
                        prev = (c, dep)
                    emit_av(*prev)
                    st[f"b_iter{it_par}"] = tT.n
                    av_t = tT.n
                    rs_t = tT.n
                    l_s = P.op("scalar", lambda e, o=rrow[:], i=prb[0:1, :]: e.activation(o, i, AF.Ln),
                               ((tT, rs_t),), sS)
                    rc_v = P.op("scalar", lambda e, o=rrow[:], i=rrow[:]: e.activation(o, i, AF.Exp, scale=-1.0),
                               ((sS, l_s),), sS)
                    for half in range(2):
                        P.op("tensor",
                             lambda e, o=prb[:, half * 512:(half + 1) * 512], r=rrow[:, half * 512:(half + 1) * 512]:
                             e.matmul(o, onesr_sb[:], r, start=True, stop=True),
                             ((sS, rc_v),) if half == 0 else (), tT if half == 1 else None)
                    bc_t = tT.n
                    st["b_avevac"] = P.op("scalar", lambda e, o=avf[:], i=pav[:]: e.copy(out=o, in_=i),
                                          ((tT, av_t), (vV, st.get("b_norm", 0))), sS)
                    av_s = st["b_avevac"]
                    dsta = attnT[:].rearrange("p (ch q) -> p ch q", ch=NCH)[:, kv * GRP:(kv + 1) * GRP, gp * 256:(gp + 1) * 256]
                    st["b_norm"] = P.op("vector",
                                        lambda e, o=dsta, a_=avf[:].rearrange("p (h q) -> p h q", h=GRP),
                                        b=prb[:].rearrange("p (h q) -> p h q", h=GRP):
                                        e.tensor_tensor(out=o, in0=a_, in1=b, op=ALU.mult),
                                        ((sS, av_s), (tT, bc_t)), vV)
            st["b_attnT_v"] = st["b_norm"]
            if attnT_dbg is not None:
                b_dbg = P.dsem("b_dbg")
                P.dma("scalar", attnT_dbg.rearrange("(ch p) q -> p ch q", p=128),
                      attnT[:].rearrange("p (ch q) -> p ch q", ch=NCH),
                      waits=((vV, st["b_attnT_v"]),), sem=b_dbg)

    if "C1" in PHASES:
        with contextlib.ExitStack() as actx:
            sba = lambda n, s, d: actx.enter_context(nc.sbuf_tensor(n, s, d))
            psa = lambda n, s, d: actx.enter_context(nc.psum_tensor(n, s, d))
            wof = sba("o_wof", [128, 2 * D], F32)
            wob = sba("o_wob", [128, 2 * D], BF16)
            oT = sba("o_oT", [128, TOK], F32)
            xcol = sba("o_xcol", [128, 16 * 128], F32)
            x1c = sba("o_x1c", [128, TOK], F32)
            pO = psa("o_pO", [128, TOK], F32)
            pxT = psa("o_pxT", [128, TOK], F32)
            o_w = [P.dsem("o_w0"), P.dsem("o_w1")]
            o_xc = P.dsem("o_xc")
            o_x1s = P.dsem("o_x1s")

            first_o = True
            for ft in range(NCH):
                wslot = ft % 2
                wsts = wof[:, wslot * D:wslot * D + D]
                wobs = wob[:, wslot * D:wslot * D + D]
                for part in range(4):
                    w0 = ()
                    if part == 0:
                        w0 = guard() if first_o else ((vV, st.get(f"o_wcast{wslot}", 0)),)
                        first_o = False
                    P.dma("sync",
                          wsts.rearrange("p (c f) -> p c f", c=NCH)[:, part * 8:(part + 1) * 8, :],
                          woT[:, ft * 128:(ft + 1) * 128].rearrange("(c p) f -> p c f", p=128)[:, part * 8:(part + 1) * 8, :],
                          waits=w0, sem=o_w[wslot], chain=(part > 0))
                wld = o_w[wslot].n
                st[f"o_wcast{wslot}"] = P.op("vector", lambda e, o=wobs, i=wsts: e.tensor_copy(out=o, in_=i),
                                             ((o_w[wslot], wld), (tT, st.get(f"o_wmm{wslot}", 0))), vV)
                wcast = st[f"o_wcast{wslot}"]
                w0 = ((vV, wcast), (vV, st.get("b_attnT_v", 0)), (sS, st.get("o_oevac", 0)))
                for c in range(NCH):
                    for tq in range(4):
                        P.op("tensor",
                             lambda e, o=pO[:, tq * 512:(tq + 1) * 512], l=wobs[:, c * 128:(c + 1) * 128],
                             r=attnT[:, c * TOK + tq * 512:c * TOK + (tq + 1) * 512],
                             s=(c == 0), t=(c == NCH - 1):
                             e.matmul(o, l, r, start=s, stop=t),
                             w0 if (c == 0 and tq == 0) else (),
                             tT if (c == NCH - 1 and tq == 3) else None)
                st[f"o_wmm{wslot}"] = tT.n
                omm = tT.n
                st["o_oevac"] = P.op("scalar", lambda e, o=oT[:], i=pO[:]: e.copy(out=o, in_=i),
                                     ((tT, omm), (vV, st.get("o_x1tt", 0))), sS)
                oev = st["o_oevac"]
                for part in range(2):
                    P.dma("sync",
                          xcol[:].rearrange("p (j f) -> p j f", f=128)[:, part * 8:(part + 1) * 8, :],
                          x_own[:, ft * 128:(ft + 1) * 128].rearrange("(j p) f -> p j f", p=128)[:, part * 8:(part + 1) * 8, :],
                          waits=((tT, st.get("o_xtr", 0)),) if part == 0 else (), sem=o_xc, chain=(part > 0))
                xld = o_xc.n
                st["o_xtr"] = transposes_128(
                    lambda j: xcol[:, j * 128:(j + 1) * 128],
                    lambda j: pxT[:, j * 128:(j + 1) * 128],
                    16, ((o_xc, xld), (vV, st.get("o_x1tt", 0))), idf_sb)
                xtr = st["o_xtr"]
                st["o_x1tt"] = P.op("vector", lambda e, o=x1c[:], a_=pxT[:], b=oT[:]:
                                    e.tensor_tensor(out=o, in0=a_, in1=b, op=ALU.add),
                                    ((tT, xtr), (sS, oev), (o_x1s, st.get("o_x1sp", 0))), vV)
                st["o_x1sp"] = P.dma("scalar", x1T_d[ft], x1c[:], waits=((vV, st["o_x1tt"]),), sem=o_x1s)
    attnT_scope.close()

    ln2T_scope = contextlib.ExitStack()
    if "C1" in PHASES:
        ln2T = ln2T_scope.enter_context(nc.sbuf_tensor("l_ln2T", [128, NCH * TOK], BF16))
        with contextlib.ExitStack() as actx:
            sba = lambda n, s, d: actx.enter_context(nc.sbuf_tensor(n, s, d))
            xc = sba("l_xc", [128, 2 * TOK], F32)
            dump2 = sba("l_dump", [128, TOK], F32)
            mrow = sba("l_mrow", [1, TOK], F32)
            sqrow = sba("l_sqrow", [1, TOK], F32)
            trow = sba("l_trow", [1, TOK], F32)
            rrow2 = sba("l_rrow2", [1, TOK], F32)
            mrrow = sba("l_mrrow", [1, TOK], F32)
            txc = sba("l_txc", [128, 1024], F32)
            g2row = sba("l_g2row", [1, 1024], BF16)
            l_x = [P.dsem("l_x0"), P.dsem("l_x1")]
            l_g2s = P.dsem("l_g2s")
            with contextlib.ExitStack() as bctx:
                psb = lambda n, s, d: bctx.enter_context(nc.psum_tensor(n, s, d))
                pmean = psb("l_pmean", [1, TOK], F32)
                psq = psb("l_psq", [1, TOK], F32)
                first_l = True
                for c in range(NCH):
                    slot = c % 2
                    xcs = xc[:, slot * TOK:(slot + 1) * TOK]
                    ld = P.dma("sync", xcs, x1T_d[c],
                               waits=(guard() if first_l else ((tT, st.get(f"l_smm{slot}", 0)),)), sem=l_x[slot])
                    first_l = False
                    for tq in range(4):
                        P.op("tensor",
                             lambda e, o=pmean[0:1, tq * 512:(tq + 1) * 512], r=xcs[:, tq * 512:(tq + 1) * 512],
                             s=(c == 0), t=(c == NCH - 1):
                             e.matmul(o, onesc_sb[:], r, start=s, stop=t),
                             ((l_x[slot], ld),) if tq == 0 else (), None)
                    sq_s = P.op("scalar", lambda e, o=dump2[:], i=xcs: e.activation(o, i, AF.Square),
                                ((l_x[slot], ld), (tT, st.get("l_sqmm", 0))), sS)
                    for tq in range(4):
                        P.op("tensor",
                             lambda e, o=psq[0:1, tq * 512:(tq + 1) * 512], r=dump2[:, tq * 512:(tq + 1) * 512],
                             s=(c == 0), t=(c == NCH - 1):
                             e.matmul(o, onesc_sb[:], r, start=s, stop=t),
                             ((sS, sq_s),) if tq == 0 else (), tT if tq == 3 else None)
                    st["l_sqmm"] = tT.n
                    st[f"l_smm{slot}"] = tT.n
                stat_t = tT.n
                mv = P.op("vector", lambda e, o=mrow[:], i=pmean[0:1, :]: e.tensor_scalar(o, i, 1.0 / D, None, ALU.mult),
                          ((tT, stat_t),), vV)
                sv = P.op("vector", lambda e, o=sqrow[:], i=psq[0:1, :]: e.tensor_scalar(o, i, 1.0 / D, None, ALU.mult),
                          ((tT, stat_t),), vV)
                m2v = P.op("vector", lambda e, o=trow[:], a_=mrow[:]: e.tensor_tensor(o, a_, a_, ALU.mult),
                           ((vV, mv),), vV)
                vvv = P.op("vector", lambda e, o=trow[:], a_=sqrow[:], b=trow[:]: e.tensor_tensor(o, a_, b, ALU.subtract),
                           ((vV, m2v), (vV, sv)), vV)
                l_s = P.op("scalar", lambda e, o=trow[:], i=trow[:]: e.activation(o, i, AF.Ln, bias=EPS),
                           ((vV, vvv),), sS)
                r_s = P.op("scalar", lambda e, o=rrow2[:], i=trow[:]: e.activation(o, i, AF.Exp, scale=-0.5),
                           ((sS, l_s),), sS)
                mrv = P.op("vector", lambda e, o=mrrow[:], i=mrow[:], r=rrow2[:]:
                           e.scalar_tensor_tensor(o, i, -1.0, r, ALU.mult, ALU.mult),
                           ((sS, r_s),), vV)
                st["l_rows_v"] = mrv
            with contextlib.ExitStack() as bctx:
                psb = lambda n, s, d: bctx.enter_context(nc.psum_tensor(n, s, d))
                rbc = psb("l_rbc", [128, 1024], F32)
                mrbc = psb("l_mrbc", [128, 1024], F32)
                pg2 = psb("l_pg2", [1, 2 * 1024], F32)
                p2_guard = ((sS, sS.n), (tT, tT.n), (vV, vV.n))
                first_p2 = True
                for th in range(2):
                    for half in range(2):
                        P.op("tensor",
                             lambda e, o=rbc[:, half * 512:(half + 1) * 512],
                             r=rrow2[:, th * 1024 + half * 512:th * 1024 + (half + 1) * 512]:
                             e.matmul(o, onesr_sb[:], r, start=True, stop=True),
                             ((vV, st["l_rows_v"]), (vV, st.get("l_tt1", 0))) if half == 0 else (), None)
                        P.op("tensor",
                             lambda e, o=mrbc[:, half * 512:(half + 1) * 512],
                             r=mrrow[:, th * 1024 + half * 512:th * 1024 + (half + 1) * 512]:
                             e.matmul(o, onesr_sb[:], r, start=True, stop=True),
                             ((vV, st.get("l_tt2", 0)),) if half == 0 else (), tT if half == 1 else None)
                    bc_t = tT.n
                    for c in range(NCH):
                        slot = c % 2
                        xcs = xc[:, slot * TOK:slot * TOK + 1024]
                        ld = P.dma("sync", xcs, x1T_d[c][:, th * 1024:(th + 1) * 1024],
                                   waits=(p2_guard if first_p2 else ((vV, st.get(f"l_n1{slot}", 0)),)),
                                   sem=l_x[slot])
                        first_p2 = False
                        st["l_tt1"] = P.op("vector", lambda e, o=txc[:], a_=xcs, b=rbc[:]:
                                           e.tensor_tensor(out=o, in0=a_, in1=b, op=ALU.mult),
                                           ((l_x[slot], ld), (tT, bc_t), (vV, st.get("l_tt2", 0))), vV)
                        st[f"l_n1{slot}"] = st["l_tt1"]
                        st["l_tt2"] = P.op("vector",
                                           lambda e, o=ln2T[:, c * TOK + th * 1024:c * TOK + (th + 1) * 1024],
                                           a_=txc[:], b=mrbc[:]:
                                           e.tensor_tensor(out=o, in0=a_, in1=b, op=ALU.add),
                                           ((vV, st["l_tt1"]),), vV)
                        g2slot = (c % 2)
                        pgs = pg2[0:1, g2slot * 1024:(g2slot + 1) * 1024]
                        for half in range(2):
                            P.op("tensor",
                                 lambda e, o=pgs[:, half * 512:(half + 1) * 512],
                                 r=ln2T[:, c * TOK + th * 1024 + half * 512:c * TOK + th * 1024 + (half + 1) * 512]:
                                 e.matmul(o, onescb_sb[:], r, start=True, stop=True),
                                 ((vV, st["l_tt2"]), (sS, st.get(f"l_g2e{g2slot}", 0))) if half == 0 else (),
                                 tT if half == 1 else None)
                        gmm = tT.n
                        ge = P.op("scalar", lambda e, o=g2row[:], i=pgs: e.copy(out=o, in_=i),
                                  ((tT, gmm), (l_g2s, st.get("l_g2q", 0))), sS)
                        st[f"l_g2e{g2slot}"] = ge
                        st["l_g2q"] = P.dma("scalar", ln2g_d[c:c + 1, th * 1024:(th + 1) * 1024], g2row[:],
                                            waits=((sS, ge),), sem=l_g2s)

    if "C2" in PHASES:
        with contextlib.ExitStack() as actx:
            sba = lambda n, s, d: actx.enter_context(nc.sbuf_tensor(n, s, d))
            psa = lambda n, s, d: actx.enter_context(nc.psum_tensor(n, s, d))
            wgb = sba("f_wgb", [128, 2 * D], BF16)
            wub = sba("f_wub", [128, 2 * D], BF16)
            zgEO = sba("f_zgEO", [2 * NCH, II // 2], BF16)
            zuEO = sba("f_zuEO", [2 * NCH, II // 2], BF16)
            l2gb = sba("f_l2gb", [2 * NCH, TOK], BF16)
            sg = sba("f_sg", [128, TOK], BF16)
            hb = sba("f_hb", [128, 2 * TOK], BF16)
            hgrow = sba("f_hgrow", [1, TOK], BF16)
            pg = psa("f_pg", [128, TOK], F32)
            pu = psa("f_pu", [128, TOK], F32)

            f_zl = P.dsem("f_zl")
            f_wg = [P.dsem("f_wg0"), P.dsem("f_wg1")]
            f_wu = [P.dsem("f_wu0"), P.dsem("f_wu1")]
            f_hs = P.dsem("f_hs")
            f_hgs = P.dsem("f_hgs")
            P.dma("sync", zgEO[:], zsmg_d, waits=guard(), sem=f_zl)
            P.dma("sync", zuEO[:], zsmu_d, sem=f_zl, chain=True)
            P.dma("sync", l2gb[0:NCH, :], ln2g_d, sem=f_zl, chain=True)
            lg = P.dma("sync", l2gb[NCH:2 * NCH, :], ln2g_d, sem=f_zl, chain=True)
            zs_ld = lg

            for it in range(NIT):
                slot = it % 2
                wgs = wgb[:, slot * D:slot * D + D]
                wus = wub[:, slot * D:slot * D + D]
                lw1 = P.dma("sync", wgs.rearrange("p (c i) -> p c i", i=128), WgB_d[it],
                            waits=((tT, st.get(f"f_gmm{slot}", 0)),), sem=f_wg[slot])
                lw2 = P.dma("sync", wus.rearrange("p (c i) -> p c i", i=128), WuB_d[it],
                            waits=((tT, st.get(f"f_umm{slot}", 0)),), sem=f_wu[slot])
                zg_t = zgEO[(it % 2) * NCH:(it % 2 + 1) * NCH, (it // 2) * 128:(it // 2 + 1) * 128]
                zu_t = zuEO[(it % 2) * NCH:(it % 2 + 1) * NCH, (it // 2) * 128:(it // 2 + 1) * 128]
                w0 = ((f_wg[slot], lw1), (f_zl, zs_ld), (sS, st.get("f_hgevac", 0)),
                      (sS, st.get("f_sil", 0)))
                for c in range(NCH):
                    for tq in range(4):
                        P.op("tensor",
                             lambda e, o=pg[:, tq * 512:(tq + 1) * 512], l=wgs[:, c * 128:(c + 1) * 128],
                             r=ln2T[:, c * TOK + tq * 512:c * TOK + (tq + 1) * 512], s=(c == 0):
                             e.matmul(o, l, r, start=s, stop=False),
                             w0 if (c == 0 and tq == 0) else (), None)
                for tq in range(4):
                    P.op("tensor",
                         lambda e, o=pg[:, tq * 512:(tq + 1) * 512], l=zg_t,
                         r=l2gb[(it % 2) * NCH:(it % 2 + 1) * NCH, tq * 512:(tq + 1) * 512]:
                         e.matmul(o, l, r, start=False, stop=True),
                         (), tT if tq == 3 else None)
                st[f"f_gmm{slot}"] = tT.n
                gmm = tT.n
                sil = P.op("scalar", lambda e, o=sg[:], i=pg[:]: e.activation(o, i, AF.Silu),
                           ((tT, gmm), (vV, st.get("f_htt", 0))), sS)
                st["f_sil"] = sil
                if it > 0:
                    st["f_emit_hg"](it - 1)
                w0 = ((f_wu[slot], lw2), (vV, st.get("f_htt", 0)))
                for c in range(NCH):
                    for tq in range(4):
                        P.op("tensor",
                             lambda e, o=pu[:, tq * 512:(tq + 1) * 512], l=wus[:, c * 128:(c + 1) * 128],
                             r=ln2T[:, c * TOK + tq * 512:c * TOK + (tq + 1) * 512], s=(c == 0):
                             e.matmul(o, l, r, start=s, stop=False),
                             w0 if (c == 0 and tq == 0) else (), None)
                for tq in range(4):
                    P.op("tensor",
                         lambda e, o=pu[:, tq * 512:(tq + 1) * 512], l=zu_t,
                         r=l2gb[(it % 2) * NCH:(it % 2 + 1) * NCH, tq * 512:(tq + 1) * 512]:
                         e.matmul(o, l, r, start=False, stop=True),
                         (), tT if tq == 3 else None)
                st[f"f_umm{slot}"] = tT.n
                umm = tT.n
                hslot = hb[:, (it % 2) * TOK:(it % 2 + 1) * TOK]
                htt = P.op("vector", lambda e, o=hslot, a_=sg[:], b=pu[:]:
                           e.tensor_tensor(out=o, in0=a_, in1=b, op=ALU.mult),
                           ((sS, sil), (tT, umm), (f_hs, st.get(f"f_hsp{it % 2}", 0)),
                            (tT, st.get(f"f_hgm{it % 2}", 0))), vV)
                st["f_htt"] = htt
                st[f"f_htt{it % 2}"] = htt
                st[f"f_hsp{it % 2}"] = P.dma("scalar", hT_d[it], hslot, waits=((vV, htt),), sem=f_hs)

                def emit_hg(jt):
                    hs = hb[:, (jt % 2) * TOK:(jt % 2 + 1) * TOK]
                    for tq in range(4):
                        P.op("tensor",
                             lambda e, o=pg[0:1, tq * 512:(tq + 1) * 512], r=hs[:, tq * 512:(tq + 1) * 512]:
                             e.matmul(o, onescb_sb[:], r, start=True, stop=True),
                             ((vV, st[f"f_htt{jt % 2}"]), (sS, st.get("f_sil", 0)),
                              (sS, st.get("f_hgevac", 0))) if tq == 0 else (),
                             tT if tq == 3 else None)
                    st[f"f_hgm{jt % 2}"] = tT.n
                    hgm = tT.n
                    hge = P.op("scalar", lambda e, o=hgrow[:], i=pg[0:1, :]: e.copy(out=o, in_=i),
                               ((tT, hgm), (f_hgs, st.get("f_hgq", 0))), sS)
                    st["f_hgevac"] = hge
                    st["f_hgq"] = P.dma("scalar", hg_d[jt:jt + 1, :], hgrow[:], waits=((sS, hge),), sem=f_hgs)
                st["f_emit_hg"] = emit_hg
                if it == NIT - 1:
                    emit_hg(it)
    ln2T_scope.close()

    if "C3" in PHASES:
        with contextlib.ExitStack() as actx:
            sba = lambda n, s, d: actx.enter_context(nc.sbuf_tensor(n, s, d))
            psa = lambda n, s, d: actx.enter_context(nc.psum_tensor(n, s, d))
            hq = sba("d_hq", [128, NIT * 512], BF16)
            wdb = sba("d_wdb", [128, 2 * NIT * 128], BF16)
            hgqb = sba("d_hgqb", [NIT, TOK], BF16)
            zsd = sba("d_zsd", [NIT, D], BF16)
            x1q = sba("d_x1q", [128, 2 * 512], F32)
            ob = sba("d_ob", [128, 2 * 512], F32)
            otile = sba("d_otile", [128, 2 * 128], F32)
            pd = psa("d_pd", [128, 2 * 512], F32)
            pout = psa("d_pout", [128, 2 * 512], F32)

            d_zl = P.dsem("d_zl")
            d_hq = P.dsem("d_hq")
            d_w = [P.dsem("d_w0"), P.dsem("d_w1")]
            d_x = [P.dsem("d_x0"), P.dsem("d_x1")]
            d_o = [P.dsem("d_o0"), P.dsem("d_o1")]
            P.dma("sync", zsd[:], zsmd_d, waits=guard(), sem=d_zl)
            l2 = P.dma("sync", hgqb[:], hg_d, sem=d_zl, chain=True)
            zs_ld = l2

            for q in range(4):
                for it in range(NIT):
                    P.dma("sync", hq[:, it * 512:(it + 1) * 512], hT_d[it][:, q * 512:(q + 1) * 512],
                          waits=((tT, st.get("d_lastmm", 0)),) if it == 0 else (),
                          sem=d_hq, chain=(it > 0))
                hq_ld = d_hq.n
                for ft in range(NCH):
                    wslot = ft % 2
                    wds = wdb[:, wslot * NIT * 128:(wslot + 1) * NIT * 128]
                    lw = P.dma("sync", wds.rearrange("p (it i) -> p it i", i=128), WdB_d[ft],
                               waits=((tT, st.get(f"d_wmm{wslot}", 0)),), sem=d_w[wslot])
                    slot = ft % 2
                    pds = pd[:, slot * 512:(slot + 1) * 512]
                    w0 = ((d_w[wslot], lw), (d_zl, zs_ld), (d_hq, hq_ld),
                          (vV, st.get(f"d_ott{slot}", 0)))
                    for it in range(NIT):
                        P.op("tensor",
                             lambda e, o=pds, l=wds[:, it * 128:(it + 1) * 128],
                             r=hq[:, it * 512:(it + 1) * 512], s=(it == 0):
                             e.matmul(o, l, r, start=s, stop=False),
                             w0 if it == 0 else (),
                             None)
                    P.op("tensor",
                         lambda e, o=pds, l=zsd[:, ft * 128:(ft + 1) * 128],
                         r=hgqb[:, q * 512:(q + 1) * 512]:
                         e.matmul(o, l, r, start=False, stop=True),
                         (), tT)
                    st[f"d_wmm{wslot}"] = tT.n
                    st["d_lastmm"] = tT.n
                    dmm = tT.n
                    xld = P.dma("sync", x1q[:, slot * 512:(slot + 1) * 512], x1T_d[ft][:, q * 512:(q + 1) * 512],
                                waits=((vV, st.get(f"d_ott{slot}", 0)),), sem=d_x[slot])
                    ott = P.op("vector",
                               lambda e, o=ob[:, slot * 512:(slot + 1) * 512], a_=pds,
                               b=x1q[:, slot * 512:(slot + 1) * 512]:
                               e.tensor_tensor(out=o, in0=a_, in1=b, op=ALU.add),
                               ((tT, dmm), (d_x[slot], xld), (tT, st.get(f"d_tr{slot}", 0))), vV)
                    st[f"d_ott{slot}"] = ott
                    tr = transposes_128(
                        lambda tau, s=slot: ob[:, s * 512 + tau * 128:s * 512 + (tau + 1) * 128],
                        lambda tau, s=slot: pout[:, s * 512 + tau * 128:s * 512 + (tau + 1) * 128],
                        4, ((vV, ott), (sS, st.get(f"d_oev{slot}", 0))), idf_sb)
                    st[f"d_tr{slot}"] = tr
                    for tau in range(4):
                        oslot = tau % 2
                        oev = P.op("scalar",
                                   lambda e, o=otile[:, oslot * 128:(oslot + 1) * 128],
                                   i=pout[:, slot * 512 + tau * 128:slot * 512 + (tau + 1) * 128]:
                                   e.copy(out=o, in_=i),
                                   ((tT, tr), (d_o[oslot], st.get(f"d_oq{oslot}", 0))), sS)
                        st[f"d_oev{slot}"] = oev
                        st[f"d_oq{oslot}"] = P.dma(
                            "scalar", out_d[(q * 4 + tau) * 128:(q * 4 + tau + 1) * 128, ft * 128:(ft + 1) * 128],
                            otile[:, oslot * 128:(oslot + 1) * 128],
                            waits=((sS, oev),), sem=d_o[oslot])

    P.evts.append(("sync", lambda e: e.nop(), P.dma_waits(), None, 1))

    for s in P.sems:
        s.h = ctx.enter_context(nc.semaphore(s.name))
    with ctx, nc.Block() as block:
        block.sync(lambda e: P.replay("sync", e))
        block.scalar(lambda e: P.replay("scalar", e))
        block.vector(lambda e: P.replay("vector", e))
        block.tensor(lambda e: P.replay("tensor", e))
        block.gpsimd(lambda e: P.replay("gpsimd", e))
    return nc


_NC_CACHE = {}


def make_in_maps(inputs):
    x = np.asarray(inputs["L0_attn_hidden_states"], dtype=np.float32)
    Wq = np.asarray(inputs["L0_attn_Wq"], np.float32)
    Wk = np.asarray(inputs["L0_attn_Wk"], np.float32)
    Wv = np.asarray(inputs["L0_attn_Wv"], np.float32)
    Wo = np.asarray(inputs["L0_attn_Wo"], np.float32)
    qg = np.asarray(inputs["L0_ffn_qw_gate"], np.int32)
    qu = np.asarray(inputs["L0_ffn_qw_up"], np.int32)
    qd = np.asarray(inputs["L0_ffn_qw_down"], np.int32)
    sg = np.asarray(inputs["L0_ffn_s_gate"], np.float32)
    zg = np.asarray(inputs["L0_ffn_z_gate"], np.float32)
    su = np.asarray(inputs["L0_ffn_s_up"], np.float32)
    zu = np.asarray(inputs["L0_ffn_z_up"], np.float32)
    sd = np.asarray(inputs["L0_ffn_s_down"], np.float32)
    zd = np.asarray(inputs["L0_ffn_z_down"], np.float32)

    T = lambda a: np.ascontiguousarray(a.T)
    cos, sin = _rope_tables()
    rot = np.zeros((HD, HD), np.float32)
    rot[:64, 64:] = -np.eye(64)
    rot[64:, :64] = np.eye(64)
    shared = {
        "wqT": T(Wq), "wkT": T(Wk), "wvT": T(Wv), "woT": T(Wo),
        "qgN": qg, "quN": qu, "qdN": qd,
        "sgP": np.ascontiguousarray(sg.reshape(NIT, 128, NCH).transpose(1, 0, 2).reshape(128, NIT * NCH)),
        "suP": np.ascontiguousarray(su.reshape(NIT, 128, NCH).transpose(1, 0, 2).reshape(128, NIT * NCH)),
        "sdP": np.ascontiguousarray(sd.reshape(NCH, 128, NIT).transpose(1, 0, 2).reshape(128, NCH * NIT)),
        "sgT": T(sg), "zgT": T(zg), "suT": T(su), "zuT": T(zu),
        "sdT": T(sd), "zdT": T(zd),
        "cosK": T(cos), "sinK": T(sin),
        "rotT": np.ascontiguousarray(rot.T).astype(BFD),
        "idb": np.eye(128, dtype=BFD), "idf": np.eye(128, dtype=np.float32),
        "onesc": np.ones((128, 1), np.float32), "onescb": np.ones((128, 1), BFD),
        "onesr": np.ones((1, 128), np.float32), "onesrb": np.ones((1, 128), BFD),
        "constv": np.tile(np.array([[EPS, SHIFT, 0, 0, 0, 0, 0, 0]], np.float32), (128, 1)),
    }
    cosKT, sinKT = shared["cosK"], shared["sinK"]
    masks = [_masks_for_parity(0), _masks_for_parity(1)]

    in_maps = []
    for core in range(NCORES):
        b, p = core // 2, core % 2
        xb = x[b]
        m = dict(shared)
        m["x_batch"] = xb
        m["x_own"] = np.ascontiguousarray(xb.reshape(32, 128, D)[p::2].reshape(TOK, D))
        m["cosQ"] = np.ascontiguousarray(cosKT.reshape(HD, 32, 128)[:, p::2].reshape(HD, TOK))
        m["sinQ"] = np.ascontiguousarray(sinKT.reshape(HD, 32, 128)[:, p::2].reshape(HD, TOK))
        m["qmask"] = masks[p]
        in_maps.append(m)
    return in_maps


def kernel(**inputs):
    in_maps = make_in_maps(inputs)
    if "nc" not in _NC_CACHE:
        _NC_CACHE["nc"] = build_nc()
    nc = _NC_CACHE["nc"]
    res = run_bass_kernel_spmd(nc, in_maps, core_ids=list(range(NCORES)))

    out = np.empty((NB, SEQ, D), np.float32)
    for core in range(NCORES):
        b, p = core // 2, core % 2
        oc = res.results[core].get("out")
        if oc is not None:
            out[b].reshape(32, 128, D)[p::2] = oc.reshape(16, 128, D)
    kernel.last_results = res.results
    return out


# version 09
# speedup vs baseline: 1.0281x; 1.0281x over previous
import contextlib
import numpy as np
import ml_dtypes
import concourse.bass as bass
import concourse.mybir as mybir
from concourse.bass_utils import run_bass_kernel_spmd

F32 = mybir.dt.float32
BF16 = mybir.dt.bfloat16
I32 = mybir.dt.int32
AF = mybir.ActivationFunctionType
ALU = mybir.AluOpType
AX = mybir.AxisListType
BFD = ml_dtypes.bfloat16

D = 4096
SEQ = 4096
NB = 4
NCORES = 8
NH = 32
NKV = 8
HD = 128
GRP = NH // NKV
II = 11008
NIT = II // 128
NCH = D // 128
TOK = 2048
EPS = 1e-5
SM_SCALE = 1.0 / float(np.sqrt(np.float32(HD)))
SHIFT = -20.0

TAPS = ()
PHASES = ("A1", "A2", "C0", "B", "C1", "C2", "C3")
C0_PARTS = ("zsm", "gu", "down")
C0_DOWN_LVL = 3


class SemC:
    __slots__ = ("name", "h", "n")

    def __init__(self, name):
        self.name = name
        self.h = None
        self.n = 0


class Prog:

    def __init__(self):
        self.evts = []
        self.sems = []
        self.dsems = []

    def sem(self, name):
        s = SemC(name)
        self.sems.append(s)
        return s

    def dsem(self, name):
        s = self.sem(name)
        self.dsems.append(s)
        return s

    def op(self, eng, fn, waits=(), inc=None, inc_k=1):
        w = tuple((s, v) for (s, v) in waits if v > 0)
        self.evts.append((eng, fn, w, inc, inc_k))
        if inc is not None:
            inc.n += inc_k
        return inc.n if inc is not None else None

    def dma(self, eng, out_ap, in_ap, waits=(), sem=None, chain=False):
        w = tuple(waits)
        if not chain and sem.n > 0:
            w = w + ((sem, sem.n),)

        def fn(e, o=out_ap, i=in_ap):
            return e.dma_start(out=o, in_=i)
        return self.op(eng, fn, w, sem, 16)

    def dma_waits(self):
        return tuple((s, s.n) for s in self.dsems if s.n > 0)

    def replay(self, eng_name, eng_obj):
        for (eng, fn, waits, inc, k) in self.evts:
            if eng != eng_name:
                continue
            for (s, v) in waits:
                eng_obj.wait_ge(s.h, v)
            ins = fn(eng_obj)
            if inc is not None:
                ins.then_inc(inc.h, k)


def _rope_tables():
    inv_freq = (1.0 / (10000.0 ** (np.arange(0, HD, 2, dtype=np.float32) / np.float32(HD)))).astype(np.float32)
    t = np.arange(SEQ, dtype=np.float32)
    freqs = np.outer(t, inv_freq).astype(np.float32)
    emb = np.concatenate([freqs, freqs], axis=-1)
    return np.cos(emb).astype(np.float32), np.sin(emb).astype(np.float32)


def _masks_for_parity(p):
    ones = np.ones((128, 128), np.float32)
    diag = np.triu(ones)
    zero = np.zeros((128, 128), np.float32)
    out = np.empty((8, 128, 128), np.float32)
    for i in range(4):
        for tau in range(2):
            d = 3 - i - 2 * tau - p
            out[i * 2 + tau] = ones if d < 0 else (diag if d == 0 else zero)
    return out.astype(BFD)


def build_nc():
    nc = bass.Bass("TRN2", target_bir_lowering=False)
    P = Prog()

    def dram(name, shape, dtype, kind="Internal"):
        if kind == "Internal" and name in TAPS:
            kind = "ExternalOutput"
        return nc.dram_tensor(name, shape, dtype, kind=kind).ap()

    x_batch = dram("x_batch", [SEQ, D], F32, "ExternalInput")
    x_own = dram("x_own", [TOK, D], F32, "ExternalInput")
    wqT = dram("wqT", [D, D], F32, "ExternalInput")
    wkT = dram("wkT", [D, NKV * HD], F32, "ExternalInput")
    wvT = dram("wvT", [D, NKV * HD], F32, "ExternalInput")
    woT = dram("woT", [D, D], F32, "ExternalInput")
    qgN = dram("qgN", [II, D], I32, "ExternalInput")
    quN = dram("quN", [II, D], I32, "ExternalInput")
    qdN = dram("qdN", [D, II], I32, "ExternalInput")
    sgP = dram("sgP", [128, NIT * NCH], F32, "ExternalInput")
    suP = dram("suP", [128, NIT * NCH], F32, "ExternalInput")
    sdP = dram("sdP", [128, NCH * NIT], F32, "ExternalInput")
    sgT = dram("sgT", [NCH, II], F32, "ExternalInput")
    zgT = dram("zgT", [NCH, II], F32, "ExternalInput")
    suT = dram("suT", [NCH, II], F32, "ExternalInput")
    zuT = dram("zuT", [NCH, II], F32, "ExternalInput")
    sdT = dram("sdT", [NIT, D], F32, "ExternalInput")
    zdT = dram("zdT", [NIT, D], F32, "ExternalInput")
    cosK = dram("cosK", [HD, SEQ], F32, "ExternalInput")
    sinK = dram("sinK", [HD, SEQ], F32, "ExternalInput")
    cosQ = dram("cosQ", [HD, TOK], F32, "ExternalInput")
    sinQ = dram("sinQ", [HD, TOK], F32, "ExternalInput")
    rotT = dram("rotT", [HD, HD], BF16, "ExternalInput")
    idb = dram("idb", [128, 128], BF16, "ExternalInput")
    idf = dram("idf", [128, 128], F32, "ExternalInput")
    onesc = dram("onesc", [128, 1], F32, "ExternalInput")
    onescb = dram("onescb", [128, 1], BF16, "ExternalInput")
    onesr = dram("onesr", [1, 128], F32, "ExternalInput")
    onesrb = dram("onesrb", [1, 128], BF16, "ExternalInput")
    qmask = dram("qmask", [8, 128, 128], BF16, "ExternalInput")
    constv = dram("constv", [128, 8], F32, "ExternalInput")

    out_d = dram("out", [TOK, D], F32, "ExternalOutput")

    ln1T_all_d = dram("ln1T_all_d", [NCH, 128, SEQ], BF16)
    KT_d = dram("KT_d", [NKV, 32, 128, 128], BF16)
    V_d = dram("V_d", [NKV, 32, 128, 128], BF16)
    QT_d = dram("QT_d", [NH * HD, TOK], BF16)
    x1T_d = dram("x1T_d", [NCH, 128, TOK], F32)
    ln2g_d = dram("ln2g_d", [NCH, TOK], BF16)
    hg_d = dram("hg_d", [NIT, TOK], BF16)
    hT_d = dram("hT_d", [NIT, 128, TOK], BF16)
    WgB_d = dram("WgB_d", [NIT, 128, NCH, 128], BF16)
    WuB_d = dram("WuB_d", [NIT, 128, NCH, 128], BF16)
    WdB_d = dram("WdB_d", [NCH, 128, NIT, 128], BF16)
    zsmg_d = dram("zsmg_d", [2 * NCH, II // 2], BF16)
    zsmu_d = dram("zsmu_d", [2 * NCH, II // 2], BF16)
    zsmd_d = dram("zsmd_d", [NIT, D], BF16)
    attnT_dbg = dram("attnT_dbg", [NH * HD, TOK], BF16) if "attnT_dbg" in TAPS else None

    tT = P.sem("tT")
    vV = P.sem("vV")
    sS = P.sem("sS")
    gG = P.sem("gG")

    ctx = contextlib.ExitStack()

    rot_sb = ctx.enter_context(nc.sbuf_tensor("rot_sb", [HD, HD], BF16))
    idb_sb = ctx.enter_context(nc.sbuf_tensor("idb_sb", [128, 128], BF16))
    idf_sb = ctx.enter_context(nc.sbuf_tensor("idf_sb", [128, 128], F32))
    onesc_sb = ctx.enter_context(nc.sbuf_tensor("onesc_sb", [128, 1], F32))
    onescb_sb = ctx.enter_context(nc.sbuf_tensor("onescb_sb", [128, 1], BF16))
    onesr_sb = ctx.enter_context(nc.sbuf_tensor("onesr_sb", [1, 128], F32))
    onesrb_sb = ctx.enter_context(nc.sbuf_tensor("onesrb_sb", [1, 128], BF16))
    qm_sb = ctx.enter_context(nc.sbuf_tensor("qm_sb", [128, 8 * 128], BF16))
    cv_sb = ctx.enter_context(nc.sbuf_tensor("cv_sb", [128, 8], F32))
    nc.const_aps.register(F32, EPS, cv_sb[:, 0:1])
    nc.const_aps.register(F32, SHIFT, cv_sb[:, 1:2])
    cst = P.dsem("cst")
    P.dma("sync", cv_sb[:], constv, sem=cst)
    for dst, src in ((rot_sb[:], rotT), (idb_sb[:], idb), (idf_sb[:], idf),
                     (onesc_sb[:], onesc), (onescb_sb[:], onescb),
                     (onesr_sb[:], onesr), (onesrb_sb[:], onesrb),
                     (qm_sb[:].rearrange("p (m f) -> p m f", m=8), qmask.transpose([1, 0, 2]))):
        P.dma("sync", dst, src, sem=cst, chain=True)
    cst_n = cst.n

    st = {}

    def ln_tile(xb_ap, ln_out_ap, dump, smb, slot, waits_x, war_norm=0):
        sm = smb[:, slot * 8:(slot + 1) * 8]
        rs_v = P.op("vector", lambda e, o=sm[:, 0:1], i=xb_ap: e.reduce_sum(out=o, in_=i, axis=AX.X),
                    waits_x, vV)
        sqs = P.op("scalar", lambda e, o=dump[:], i=xb_ap, a=sm[:, 1:2]: e.activation(o, i, AF.Square, accum_out=a),
                   waits_x, sS)
        m_v = P.op("vector", lambda e, o=sm[:, 2:3], i=sm[:, 0:1]: e.tensor_scalar(o, i, 1.0 / D, None, ALU.mult),
                   ((vV, rs_v),), vV)
        msq_v = P.op("vector", lambda e, o=sm[:, 3:4], i=sm[:, 1:2]: e.tensor_scalar(o, i, 1.0 / D, None, ALU.mult),
                     ((sS, sqs),), vV)
        m2_v = P.op("vector", lambda e, o=sm[:, 4:5], a=sm[:, 2:3]: e.tensor_tensor(o, a, a, ALU.mult),
                    ((vV, m_v),), vV)
        var_v = P.op("vector", lambda e, o=sm[:, 4:5], a=sm[:, 3:4], b=sm[:, 4:5]: e.tensor_tensor(o, a, b, ALU.subtract),
                     ((vV, m2_v), (vV, msq_v)), vV)
        ln_s = P.op("scalar", lambda e, o=sm[:, 5:6], i=sm[:, 4:5]: e.activation(o, i, AF.Ln, bias=EPS),
                    ((vV, var_v),), sS)
        r_s = P.op("scalar", lambda e, o=sm[:, 6:7], i=sm[:, 5:6]: e.activation(o, i, AF.Exp, scale=-0.5),
                   ((sS, ln_s),), sS)
        mr_v = P.op("vector",
                    lambda e, o=sm[:, 7:8], i=sm[:, 2:3], r=sm[:, 6:7]: e.scalar_tensor_tensor(o, i, -1.0, r, ALU.mult, ALU.mult),
                    ((sS, r_s),), vV)
        P.op("vector",
             lambda e, o=ln_out_ap, i=xb_ap, r=sm[:, 6:7], m=sm[:, 7:8]: e.tensor_scalar(o, i, r, m, ALU.mult, ALU.add),
             ((tT, war_norm), (vV, mr_v)), vV)
        return vV.n

    def transposes_128(src_ap_fn, dst_psum_fn, n, first_waits, id_sb):
        for c in range(n):
            P.op("tensor",
                 lambda e, o=dst_psum_fn(c), i=src_ap_fn(c): e.transpose(o, i, id_sb[:]),
                 first_waits if c == 0 else (), tT if c == n - 1 else None)
        return tT.n

    def rope_block(kbf, pR, t1, t2, out_ap, cos_slice, sin_slice, evac_s, war_out, tab_w):
        P.op("tensor", lambda e, o=pR[:], l=rot_sb[:], r=kbf[:]: e.matmul(o, l, r, start=True, stop=True),
             ((sS, evac_s), (vV, st.get("rope_t2", 0))), tT)
        rot_t = tT.n
        t1_v = P.op("vector", lambda e, o=t1[:], i=kbf[:], cc=cos_slice: e.tensor_tensor(out=o, in0=i, in1=cc, op=ALU.mult),
                    ((sS, evac_s), tab_w, (vV, st.get("rope_add", 0))), vV)
        st["rope_t1"] = t1_v
        t2_v = P.op("vector", lambda e, o=t2[:], i=pR[:], cc=sin_slice: e.tensor_tensor(out=o, in0=i, in1=cc, op=ALU.mult),
                    ((tT, rot_t), (vV, st.get("rope_add", 0))), vV)
        st["rope_t2"] = t2_v
        add_v = P.op("vector", lambda e, o=out_ap, a_=t1[:], b=t2[:]: e.tensor_tensor(out=o, in0=a_, in1=b, op=ALU.add),
                     (war_out, (vV, t1_v), (vV, t2_v)), vV)
        st["rope_add"] = add_v
        return add_v

    if "A1" in PHASES:
        with contextlib.ExitStack() as actx:
            sba = lambda n, s, d: actx.enter_context(nc.sbuf_tensor(n, s, d))
            psa = lambda n, s, d: actx.enter_context(nc.psum_tensor(n, s, d))
            wst = sba("a_wst", [128, 1024], F32)
            wkv_bf = sba("a_wkv", [128, NCH * 1024], BF16)
            xb = sba("a_xb", [128, 2 * D], F32)
            smb = sba("a_smb", [128, 32], F32)
            lnb = sba("a_lnb", [128, 2 * D], BF16)
            gbuf = sba("a_gbuf", [128, NCH * 512], BF16)
            kbf = sba("a_kbf", [128, 2 * 512], BF16)
            t1 = sba("a_t1", [128, 512], F32)
            t2 = sba("a_t2", [128, 512], F32)
            krb = sba("a_krb", [128, 2 * 512], BF16)
            csK = sba("a_csK", [128, SEQ], F32)
            snK = sba("a_snK", [128, SEQ], F32)
            pT = psa("a_pT", [128, D], BF16)
            pK = psa("a_pK", [128, 2 * 512], F32)
            pR = psa("a_pR", [128, 2 * 512], F32)

            a_cs = P.dsem("a_cs")
            a_x = [P.dsem("a_x0"), P.dsem("a_x1")]
            a_w = P.dsem("a_w")
            a_gld = P.dsem("a_gld")
            a_gsp = P.dsem("a_gsp")
            a_kr = [P.dsem("a_kr0"), P.dsem("a_kr1")]
            P.dma("sync", csK[:], cosK, waits=((cst, cst_n),), sem=a_cs)
            P.dma("sync", snK[:], sinK, sem=a_cs, chain=True)
            cs_n = a_cs.n

            def load_wkv(src_dram):
                t0 = tT.n
                for c in range(NCH):
                    ld = P.dma("sync", wst[:], src_dram[c * 128:(c + 1) * 128, :],
                               waits=((vV, st.get("wcast", 0)),), sem=a_w)
                    P.op("vector",
                         lambda e, o=wkv_bf[:, c * 1024:(c + 1) * 1024], i=wst[:]: e.tensor_copy(out=o, in_=i),
                         ((a_w, ld), (tT, t0)), vV)
                    st["wcast"] = vV.n

            def a1_pass(which, w_ready_v):
                for a in range(8):
                    if which == "K":
                        for tau in range(4):
                            tglob = 4 * a + tau
                            slot = tau % 2
                            buf = xb[:, slot * D:slot * D + D]
                            ld = P.dma("sync", buf, x_batch[tglob * 128:(tglob + 1) * 128, :],
                                       waits=((vV, st.get(f"a_xfree{slot}", 0)),), sem=a_x[slot])
                            lslot = lnb[:, slot * D:(slot + 1) * D]
                            vln = ln_tile(buf, lslot, lslot, smb, tau, ((a_x[slot], ld), (cst, cst_n)),
                                          war_norm=st.get(f"a_lnb_tpose{slot}", 0))
                            t_aft = transposes_128(
                                lambda c: lslot[:, c * 128:(c + 1) * 128],
                                lambda c: pT[:, c * 128:(c + 1) * 128],
                                NCH, ((vV, vln), (vV, st.get("a_pT_evac", 0))), idb_sb)
                            st[f"a_lnb_tpose{slot}"] = t_aft
                            dst = gbuf[:].rearrange("p (c q) -> p c q", c=NCH)[:, :, tau * 128:(tau + 1) * 128]
                            src = pT[:].rearrange("p (c q) -> p c q", c=NCH)
                            P.op("vector", lambda e, o=dst, i=src: e.tensor_copy(out=o, in_=i),
                                 ((tT, t_aft), (a_gsp, st.get("a_gspill", 0))), vV)
                            st["a_pT_evac"] = vV.n
                            st[f"a_xfree{slot}"] = vV.n
                            st["a_gbuf_v"] = vV.n
                        P.dma("scalar",
                              ln1T_all_d[:, :, a * 512:(a + 1) * 512].transpose([1, 0, 2]),
                              gbuf[:].rearrange("p (c q) -> p c q", c=NCH),
                              waits=((vV, st["a_gbuf_v"]),), sem=a_gsp)
                        st["a_gspill"] = a_gsp.n
                        mm_w0 = ((vV, st["a_gbuf_v"]),)
                    else:
                        ld = P.dma("sync", gbuf[:].rearrange("p (c q) -> p c q", c=NCH),
                                   ln1T_all_d[:, :, a * 512:(a + 1) * 512].transpose([1, 0, 2]),
                                   waits=((tT, tT.n), (a_gsp, a_gsp.n)), sem=a_gld)
                        mm_w0 = ((a_gld, ld), (vV, w_ready_v))

                    def emit_rope_k(f, evac_s):
                        bank = f % 2
                        kslot = kbf[:, bank * 512:(bank + 1) * 512]
                        dst_slot = krb[:, bank * 512:bank * 512 + 512]
                        kv_v = rope_block(kslot, pR[:, bank * 512:(bank + 1) * 512], t1, t2, dst_slot,
                                          csK[:, st["a_cur"] * 512:(st["a_cur"] + 1) * 512],
                                          snK[:, st["a_cur"] * 512:(st["a_cur"] + 1) * 512],
                                          evac_s, (a_kr[bank], st.get(f"a_krbq{bank}", 0)),
                                          (a_cs, cs_n))
                        st[f"rope_t1b{bank}"] = st["rope_t1"]
                        P.dma("scalar",
                              KT_d[f, 4 * st["a_cur"]:4 * st["a_cur"] + 4].transpose([1, 0, 2]),
                              dst_slot.rearrange("p (b q) -> p b q", b=4),
                              waits=((vV, kv_v),), sem=a_kr[bank])
                        st[f"a_krbq{bank}"] = a_kr[bank].n

                    pending_rope = st.pop("a_pend", None)
                    st["a_cur_next"] = a
                    for f in range(NKV):
                        bank = f % 2
                        pKb = pK[:, bank * 512:(bank + 1) * 512]
                        w0 = mm_w0 + ((sS, st.get(f"a_evac{bank}", 0)),)
                        for c in range(NCH):
                            P.op("tensor",
                                 lambda e, o=pKb, l=wkv_bf[:, (c * 8 + f) * 128:(c * 8 + f + 1) * 128],
                                 r=gbuf[:, c * 512:(c + 1) * 512], s=(c == 0), t=(c == NCH - 1):
                                 e.matmul(o, l, r, start=s, stop=t),
                                 w0 if c == 0 else (), tT if c == NCH - 1 else None)
                        mm_done = tT.n
                        kslot = kbf[:, bank * 512:(bank + 1) * 512]
                        P.op("scalar", lambda e, o=kslot, i=pKb: e.copy(out=o, in_=i),
                             ((tT, mm_done), (vV, st.get(f"rope_t1b{bank}", 0))), sS)
                        st[f"a_evac{bank}"] = sS.n
                        evac_s = sS.n
                        if which == "K":
                            if pending_rope is not None:
                                st["a_cur"] = pending_rope[2]
                                emit_rope_k(pending_rope[0], pending_rope[1])
                            pending_rope = (f, evac_s, a)
                        else:
                            slot = f % 2
                            dst_slot = krb[:, slot * 512:slot * 512 + 512]
                            dummy = True
                            tr_t = transposes_128(
                                lambda b_: kslot[:, b_ * 128:(b_ + 1) * 128],
                                lambda b_: pT[:, b_ * 128:(b_ + 1) * 128],
                                4, ((sS, evac_s), (vV, st.get("a_pTv_evac", 0))), idb_sb)
                            st[f"rope_t1b{bank}"] = 0
                            P.op("vector", lambda e, o=dst_slot, i=pT[:, 0:512]: e.tensor_copy(out=o, in_=i),
                                 ((tT, tr_t), (a_kr[slot], st.get(f"a_krbq{slot}", 0))), vV)
                            st["a_pTv_evac"] = vV.n
                            kv_v = vV.n
                            P.dma("scalar",
                                  V_d[f, 4 * a:4 * a + 4].transpose([1, 0, 2]),
                                  dst_slot.rearrange("p (b q) -> p b q", b=4),
                                  waits=((vV, kv_v),), sem=a_kr[slot])
                            st[f"a_krbq{slot}"] = a_kr[slot].n
                    if which == "K":
                        st["a_pend"] = pending_rope
                        if a == 7 and pending_rope is not None:
                            st["a_cur"] = pending_rope[2]
                            emit_rope_k(pending_rope[0], pending_rope[1])
                            st.pop("a_pend")

            load_wkv(wkT)
            a1_pass("K", st["wcast"])
            load_wkv(wvT)
            a1_pass("V", st["wcast"])

    if "A2" in PHASES:
        with contextlib.ExitStack() as actx:
            sba = lambda n, s, d: actx.enter_context(nc.sbuf_tensor(n, s, d))
            psa = lambda n, s, d: actx.enter_context(nc.psum_tensor(n, s, d))
            lnT = sba("q_lnT", [128, NCH * TOK], BF16)
            csQ = sba("q_csQ", [128, TOK], F32)
            snQ = sba("q_snQ", [128, TOK], F32)
            pT = psa("q_pT", [128, D], BF16)
            pQ = psa("q_pQ", [128, 2 * 512], F32)
            pR = psa("q_pR", [128, 512], F32)

            q_cs = P.dsem("q_cs")
            q_x = [P.dsem("q_x0"), P.dsem("q_x1")]
            q_w = [P.dsem("q_w0"), P.dsem("q_w1")]
            q_qr = [P.dsem("q_qr0"), P.dsem("q_qr1")]
            P.dma("sync", csQ[:], cosQ,
                  waits=((vV, vV.n), (sS, sS.n), (tT, tT.n)) + P.dma_waits(), sem=q_cs)
            P.dma("sync", snQ[:], sinQ, sem=q_cs, chain=True)
            qcs_n = q_cs.n

            with contextlib.ExitStack() as bctx:
                sbb = lambda n, s, d: bctx.enter_context(nc.sbuf_tensor(n, s, d))
                xb = sbb("q_xb", [128, 2 * D], F32)
                smb = sbb("q_smb", [128, 32], F32)
                lnb = sbb("q_lnb", [128, 2 * D], BF16)
                for u in range(4):
                    for tau in range(4):
                        j = 4 * u + tau
                        slot = tau % 2
                        buf = xb[:, slot * D:slot * D + D]
                        ld = P.dma("sync", buf, x_own[j * 128:(j + 1) * 128, :],
                                   waits=((vV, st.get(f"q_xfree{slot}", 0)),), sem=q_x[slot])
                        lslot = lnb[:, slot * D:(slot + 1) * D]
                        vln = ln_tile(buf, lslot, lslot, smb, tau, ((q_x[slot], ld),),
                                      war_norm=st.get(f"q_lnb_tpose{slot}", 0))
                        t_aft = transposes_128(
                            lambda c: lslot[:, c * 128:(c + 1) * 128],
                            lambda c: pT[:, c * 128:(c + 1) * 128],
                            NCH, ((vV, vln), (vV, st.get("q_pT_evac", 0))), idb_sb)
                        st[f"q_lnb_tpose{slot}"] = t_aft
                        dst = lnT[:].rearrange("p (c q) -> p c q", c=NCH)[:, :, u * 512 + tau * 128:u * 512 + (tau + 1) * 128]
                        src = pT[:].rearrange("p (c q) -> p c q", c=NCH)
                        P.op("vector", lambda e, o=dst, i=src: e.tensor_copy(out=o, in_=i),
                             ((tT, t_aft),), vV)
                        st["q_pT_evac"] = vV.n
                        st[f"q_xfree{slot}"] = vV.n
            st["q_lnT_v"] = vV.n

            wqs = sba("q_wqs", [128, 2 * NCH * 128], F32)
            wqb = sba("q_wqb", [128, 2 * NCH * 128], BF16)
            kbf = sba("q_kbf", [128, 2 * 512], BF16)
            t1 = sba("q_t1", [128, 512], F32)
            t2 = sba("q_t2", [128, 512], F32)
            qrb = sba("q_qrb", [128, 2 * 512], BF16)

            for f in range(NH):
                wslot = f % 2
                wsts = wqs[:, wslot * D:wslot * D + D]
                wqbs = wqb[:, wslot * D:wslot * D + D]
                for part in range(4):
                    P.dma("sync",
                          wsts.rearrange("p (c f) -> p c f", c=NCH)[:, part * 8:(part + 1) * 8, :],
                          wqT[:, f * 128:(f + 1) * 128].rearrange("(c p) f -> p c f", p=128)[:, part * 8:(part + 1) * 8, :],
                          waits=((vV, max(st.get(f"q_wcast{wslot}", 0), st["q_lnT_v"])),) if part == 0 else (),
                          sem=q_w[wslot], chain=(part > 0))
                wq_ld = q_w[wslot].n
                P.op("vector", lambda e, o=wqbs, i=wsts: e.tensor_copy(out=o, in_=i),
                     ((q_w[wslot], wq_ld), (tT, st.get(f"q_wmm{wslot}", 0))), vV)
                st[f"q_wcast{wslot}"] = vV.n
                wcast = vV.n
                for u in range(4):
                    bank = u % 2
                    pQb = pQ[:, bank * 512:(bank + 1) * 512]
                    w0 = ((vV, wcast), (sS, st.get(f"q_evac{bank}", 0)))
                    for c in range(NCH):
                        P.op("tensor",
                             lambda e, o=pQb, l=wqbs[:, c * 128:(c + 1) * 128],
                             r=lnT[:, c * TOK + u * 512:c * TOK + (u + 1) * 512],
                             s=(c == 0), t=(c == NCH - 1):
                             e.matmul(o, l, r, start=s, stop=t),
                             w0 if c == 0 else (), tT if c == NCH - 1 else None)
                    st[f"q_wmm{wslot}"] = tT.n
                    mm_done = tT.n
                    kslot = kbf[:, bank * 512:(bank + 1) * 512]
                    P.op("scalar", lambda e, o=kslot, i=pQb: e.copy(out=o, in_=i),
                         ((tT, mm_done), (vV, st.get(f"rope_t1b{bank}", 0))), sS)
                    st[f"q_evac{bank}"] = sS.n
                    evac_s = sS.n
                    slot = u % 2
                    dst_slot = qrb[:, slot * 512:slot * 512 + 512]
                    kv_v = rope_block(kslot, pR, t1, t2, dst_slot,
                                      csQ[:, u * 512:(u + 1) * 512],
                                      snQ[:, u * 512:(u + 1) * 512],
                                      evac_s, (q_qr[slot], st.get(f"q_qrbq{slot}", 0)),
                                      (q_cs, qcs_n))
                    st[f"rope_t1b{bank}"] = st["rope_t1"]
                    P.dma("scalar", QT_d[f * 128:(f + 1) * 128, u * 512:(u + 1) * 512], dst_slot,
                          waits=((vV, kv_v),), sem=q_qr[slot])
                    st[f"q_qrbq{slot}"] = q_qr[slot].n

    def guard():
        return ((vV, vV.n), (sS, sS.n), (tT, tT.n), (gG, gG.n)) + P.dma_waits()

    if "C0" in PHASES:
        with contextlib.ExitStack() as actx:
            sba = lambda n, s, d: actx.enter_context(nc.sbuf_tensor(n, s, d))
            psa = lambda n, s, d: actx.enter_context(nc.psum_tensor(n, s, d))
            pT0 = psa("c0_pT0", [128, D], BF16)
            c_sz = P.dsem("c_sz")
            c_zsp = P.dsem("c_zsp")
            c_qb = [P.dsem("c_qb0"), P.dsem("c_qb1")]
            c_ws = [P.dsem("c_ws0"), P.dsem("c_ws1")]

            with contextlib.ExitStack() as bctx:
              if "zsm" in C0_PARTS:
                  sbb = lambda n, s, d: bctx.enter_context(nc.sbuf_tensor(n, s, d))
                  s_st = sbb("c0_s", [NCH, II], F32)
                  z_st = sbb("c0_z", [NCH, II], F32)
                  zs_sb = sbb("c0_zs", [NCH, II], BF16)
                  sd_st = sbb("c0_sd", [NIT, D], F32)
                  zd_st = sbb("c0_zd", [NIT, D], F32)
                  zsd_sb = sbb("c0_zsd", [NIT, D], BF16)
                  first = True
                  for (s_dram, z_dram, zdst) in ((sgT, zgT, zsmg_d), (suT, zuT, zsmu_d)):
                      w0 = guard() if first else ((c_zsp, c_zsp.n),)
                      first = False
                      P.dma("sync", s_st[:], s_dram, waits=w0, sem=c_sz)
                      l2 = P.dma("sync", z_st[:], z_dram, sem=c_sz, chain=True)
                      vz = []
                      for par in range(2):
                          src_z = z_st[:].rearrange("g (j two i) -> g j two i", two=2, i=128)[:, :, par, :]
                          src_s = s_st[:].rearrange("g (j two i) -> g j two i", two=2, i=128)[:, :, par, :]
                          o = zs_sb[:, par * (II // 2):(par + 1) * (II // 2)].rearrange("g (j i) -> g j i", i=128)
                          vz.append(P.op("vector",
                                         lambda e, o=o, a_=src_z, b=src_s: e.scalar_tensor_tensor(o, a_, -1.0, b, ALU.mult, ALU.mult),
                                         ((c_sz, l2),), vV))
                      P.dma("scalar", zdst.rearrange("(two g) i -> g two i", g=NCH),
                            zs_sb[:].rearrange("g (two i) -> g two i", two=2),
                            waits=((vV, vz[-1]),), sem=c_zsp)
                  P.dma("sync", sd_st[:], sdT, waits=((c_zsp, c_zsp.n),), sem=c_sz)
                  l2 = P.dma("sync", zd_st[:], zdT, sem=c_sz, chain=True)
                  vz = P.op("vector", lambda e, o=zsd_sb[:], a_=zd_st[:], b=sd_st[:]:
                            e.scalar_tensor_tensor(o, a_, -1.0, b, ALU.mult, ALU.mult),
                            ((c_sz, l2),), vV)
                  P.dma("scalar", zsmd_d, zsd_sb[:], waits=((vV, vz),), sem=c_zsp)

            with contextlib.ExitStack() as bctx:
              if "gu" in C0_PARTS:
                  sbb = lambda n, s, d: bctx.enter_context(nc.sbuf_tensor(n, s, d))
                  sgp_sb = sbb("c0_sgp", [128, NIT * NCH], F32)
                  sup_sb = sbb("c0_sup", [128, NIT * NCH], F32)
                  qblk = sbb("c0_qblk", [128, 2 * D], I32)
                  tsb = sbb("c0_tsb", [128, 2 * D], BF16)
                  wvout = sbb("c0_wv", [128, 2 * D], BF16)
                  P.dma("sync", sgp_sb[:], sgP, waits=((c_zsp, c_zsp.n),), sem=c_sz)
                  spl = P.dma("sync", sup_sb[:], suP, sem=c_sz, chain=True)
                  for (q_dram, sp_sb, wdst, mtag) in ((qgN, sgp_sb, WgB_d, "g"), (quN, sup_sb, WuB_d, "u")):
                      for it in range(NIT):
                          slot = it % 2
                          qb = qblk[:, slot * D:(slot + 1) * D]
                          tb = tsb[:, slot * D:(slot + 1) * D]
                          wo = wvout[:, slot * D:(slot + 1) * D]
                          qld = P.dma("sync", qb, q_dram[it * 128:(it + 1) * 128, :],
                                      waits=((vV, st.get(f"c0_ts{slot}", 0)), (gG, st.get(f"c0_tsg{slot}", 0))),
                                      sem=c_qb[slot])
                          eng, sem = ("vector", vV)
                          bc = sp_sb[:, it * NCH:(it + 1) * NCH].rearrange("p (c o) -> p c o", o=1).broadcast_to([128, NCH, 128])
                          P.op(eng,
                               lambda e, o=tb.rearrange("p (c i) -> p c i", i=128),
                               i=qb.rearrange("p (c i) -> p c i", i=128), b=bc:
                               e.tensor_tensor(out=o, in0=i, in1=b, op=ALU.mult),
                               ((c_qb[slot], qld), (c_sz, spl),
                                (tT, st.get(f"c0_tp{slot}", 0))), sem)
                          ts_v, ts_g = vV.n, gG.n
                          st[f"c0_ts{slot}"] = ts_v
                          st[f"c0_tsg{slot}"] = ts_g
                          tp = transposes_128(
                              lambda c: tb[:, c * 128:(c + 1) * 128],
                              lambda c: pT0[:, c * 128:(c + 1) * 128],
                              NCH, ((vV, ts_v), (gG, ts_g), (vV, st.get("c0_ev", 0)), (sS, st.get("c0_evs", 0))), idb_sb)
                          st[f"c0_tp{slot}"] = tp
                          half = NCH // 2
                          ev_v = P.op("vector",
                                      lambda e, o=wo[:, 0:half * 128], i=pT0[:, 0:half * 128]:
                                      e.tensor_copy(out=o, in_=i),
                                      ((tT, tp), (c_ws[slot], st.get(f"c0_wsp{slot}", 0))), vV)
                          st["c0_ev"] = ev_v
                          ev_s = P.op("scalar",
                                      lambda e, o=wo[:, half * 128:], i=pT0[:, half * 128:]:
                                      e.copy(out=o, in_=i),
                                      ((tT, tp), (c_ws[slot], st.get(f"c0_wsp{slot}", 0))), sS)
                          st["c0_evs"] = ev_s
                          P.dma("scalar", wdst[it], wo.rearrange("p (c i) -> p c i", i=128),
                                waits=((vV, ev_v), (sS, ev_s)), sem=c_ws[slot])
                          st[f"c0_wsp{slot}"] = c_ws[slot].n

            with contextlib.ExitStack() as bctx:
              if "down" in C0_PARTS:
                  sbb = lambda n, s, d: bctx.enter_context(nc.sbuf_tensor(n, s, d))
                  sdp_sb = sbb("c0_sdp", [128, NCH * NIT], F32)
                  qcol = sbb("c0_qcol", [128, 2 * II], I32)
                  tsb2 = sbb("c0_tsb2", [128, 2 * II], BF16)
                  wdout = sbb("c0_wd", [128, 2 * II], BF16)
                  sdl = P.dma("sync", sdp_sb[:], sdP, waits=((c_ws[0], c_ws[0].n), (c_ws[1], c_ws[1].n)), sem=c_sz)
                  for ft in range(NCH):
                      slot = ft % 2
                      qb = qcol[:, slot * II:(slot + 1) * II]
                      tb = tsb2[:, slot * II:(slot + 1) * II]
                      wo = wdout[:, slot * II:(slot + 1) * II]
                      qld = P.dma("sync", qb, qdN[ft * 128:(ft + 1) * 128, :],
                                  waits=((vV, st.get(f"c0_dts{slot}", 0)), (gG, st.get(f"c0_dtsg{slot}", 0))),
                                  sem=c_qb[slot])
                      for ch0 in range(0, NIT, 32):
                          nb = min(32, NIT - ch0)
                          bc = sdp_sb[:, ft * NIT + ch0:ft * NIT + ch0 + nb].rearrange("p (c o) -> p c o", o=1).broadcast_to([128, nb, 128])
                          P.op("vector",
                               lambda e, o=tb[:, ch0 * 128:(ch0 + nb) * 128].rearrange("p (c i) -> p c i", i=128),
                               i=qb[:, ch0 * 128:(ch0 + nb) * 128].rearrange("p (c i) -> p c i", i=128), b=bc:
                               e.tensor_tensor(out=o, in0=i, in1=b, op=ALU.mult),
                               ((c_qb[slot], qld), (c_sz, sdl),
                                (tT, st.get(f"c0_dtp{slot}", 0))) if ch0 == 0 else (), vV)
                      st[f"c0_dts{slot}"] = vV.n
                      st[f"c0_dtsg{slot}"] = gG.n
                      ts_v, ts_g = vV.n, gG.n
                      if C0_DOWN_LVL < 2:
                          st[f"c0_dtp{slot}"] = tT.n
                          st[f"c0_dsp{slot}"] = c_ws[slot].n
                          continue
                      for ch0 in range(0, NIT, 32):
                          nblk = min(32, NIT - ch0)
                          tp = transposes_128(
                              lambda b, base=ch0: tb[:, (base + b) * 128:(base + b + 1) * 128],
                              lambda b: pT0[:, b * 128:(b + 1) * 128],
                              nblk, ((vV, ts_v), (gG, ts_g), (vV, st.get("c0_ev", 0)), (sS, st.get("c0_evs", 0))), idb_sb)
                          st[f"c0_dtp{slot}"] = tp
                          hb_ = min(16, nblk)
                          ev_v = P.op("vector",
                                      lambda e, o=wo[:, ch0 * 128:(ch0 + hb_) * 128], i=pT0[:, 0:hb_ * 128]:
                                      e.tensor_copy(out=o, in_=i),
                                      ((tT, tp), (c_ws[slot], st.get(f"c0_dsp{slot}", 0))), vV)
                          st["c0_ev"] = ev_v
                          ev_s = P.op("scalar",
                                      lambda e, o=wo[:, (ch0 + hb_) * 128:(ch0 + nblk) * 128],
                                      i=pT0[:, hb_ * 128:nblk * 128]:
                                      e.copy(out=o, in_=i),
                                      ((tT, tp), (c_ws[slot], st.get(f"c0_dsp{slot}", 0))), sS)
                          st["c0_evs"] = ev_s
                      if C0_DOWN_LVL >= 3:
                          P.dma("scalar", WdB_d[ft], wo.rearrange("p (it i) -> p it i", i=128),
                                waits=((vV, st["c0_ev"]), (sS, st["c0_evs"])), sem=c_ws[slot])
                      st[f"c0_dsp{slot}"] = c_ws[slot].n

    attnT_scope = contextlib.ExitStack()
    if "B" in PHASES:
        attnT = attnT_scope.enter_context(nc.sbuf_tensor("b_attnT", [128, NCH * TOK], BF16))
        with contextlib.ExitStack() as actx:
            sba = lambda n, s, d: actx.enter_context(nc.sbuf_tensor(n, s, d))
            psa = lambda n, s, d: actx.enter_context(nc.psum_tensor(n, s, d))
            qbuf = sba("b_qbuf", [128, 2 * 1024], BF16)
            kbuf = sba("b_kbuf", [128, 2 * 32 * 128], BF16)
            vbuf = sba("b_vbuf", [128, 2 * 32 * 128], BF16)
            pbv = sba("b_pb", [128, 2 * 1024], BF16)
            racc = sba("b_racc", [128, 2 * 1024], F32)
            avf = sba("b_avf", [128, 1024], F32)
            rrow = sba("b_rrow", [1, 1024], F32)
            pl = psa("b_pl", [128, 2 * 1024], F32)
            pav = psa("b_pav", [128, 1024], F32)
            prb = psa("b_prb", [128, 1024], F32)

            b_q = [P.dsem("b_q0"), P.dsem("b_q1")]
            b_k = [P.dsem("b_k0"), P.dsem("b_k1")]
            b_v = [P.dsem("b_v0"), P.dsem("b_v1")]
            first_b = True
            for gp in range(8):
                E = 4 * gp + 4
                for kv in range(NKV):
                    it_par = (gp * NKV + kv) % 2
                    qb = qbuf[:, it_par * 1024:(it_par + 1) * 1024]
                    kb = kbuf[:, it_par * 4096:(it_par + 1) * 4096]
                    vb = vbuf[:, it_par * 4096:(it_par + 1) * 4096]
                    slot_guard = (tT, st.get(f"b_iter{it_par}", 0))
                    for h in range(GRP):
                        P.dma("sync", qb[:, h * 256:(h + 1) * 256],
                              QT_d[(kv * GRP + h) * 128:(kv * GRP + h + 1) * 128, gp * 256:(gp + 1) * 256],
                              waits=(guard() if first_b else (slot_guard,)) if h == 0 else (),
                              sem=b_q[it_par], chain=(h > 0))
                        first_b = False
                    q_ld = b_q[it_par].n
                    for c in range(E):
                        P.dma("sync", kb[:, c * 128:(c + 1) * 128], KT_d[kv, c],
                              sem=b_k[it_par], chain=(c > 0))
                        P.dma("sync", vb[:, c * 128:(c + 1) * 128], V_d[kv, c],
                              sem=b_v[it_par], chain=(c > 0))
                    k_ld, v_ld = b_k[it_par].n, b_v[it_par].n

                    def emit_av(c, dep):
                        psl = pbv[:, (c % 2) * 1024:(c % 2 + 1) * 1024]
                        w_av = (dep, (b_v[it_par], v_ld)) + (((sS, st.get("b_avevac", 0)),) if c == 0 else ())
                        for half in range(2):
                            P.op("tensor",
                                 lambda e, o=pav[:, half * 512:(half + 1) * 512], l=vb[:, c * 128:(c + 1) * 128],
                                 r=psl[:, half * 512:(half + 1) * 512], s=(c == 0), t=(c == E - 1):
                                 e.matmul(o, l, r, start=s, stop=t),
                                 w_av if half == 0 else (), None)

                    prev = None
                    for c in range(E):
                        lsl = pl[:, (c % 2) * 1024:(c % 2 + 1) * 1024]
                        psl = pbv[:, (c % 2) * 1024:(c % 2 + 1) * 1024]
                        w0 = ((b_k[it_par], k_ld), (b_q[it_par], q_ld), (sS, st.get(f"b_exp{c % 2}", 0)),
                              (vV, st.get(f"b_msk{c % 2}", 0)), (vV, st.get(f"b_rac{c % 2}", 0)))
                        for h in range(GRP):
                            P.op("tensor",
                                 lambda e, o=lsl[:, h * 256:(h + 1) * 256], l=kb[:, c * 128:(c + 1) * 128],
                                 r=qb[:, h * 256:(h + 1) * 256]:
                                 e.matmul(o, l, r, start=True, stop=True),
                                 w0 if h == 0 else (), tT if h == GRP - 1 else None)
                        qk_t = tT.n
                        if prev is not None:
                            emit_av(*prev)
                        ex_s = P.op("scalar",
                                    lambda e, o=psl, i=lsl: e.activation(o, i, AF.Exp, bias=SHIFT, scale=SM_SCALE),
                                    ((tT, qk_t), (vV, st.get(f"b_msk{c % 2}", 0)),
                                     (vV, st.get(f"b_rac{c % 2}", 0))), sS)
                        st[f"b_exp{c % 2}"] = ex_s
                        dep = (sS, ex_s)
                        if c >= E - 4:
                            i_ = E - 1 - c
                            mv = 0
                            for tau in range(2):
                                for h in range(GRP):
                                    mv = P.op("vector",
                                              lambda e, o=psl[:, h * 256 + tau * 128:h * 256 + (tau + 1) * 128],
                                              m=qm_sb[:, (i_ * 2 + tau) * 128:(i_ * 2 + tau + 1) * 128]:
                                              e.tensor_tensor(out=o, in0=o, in1=m, op=ALU.mult),
                                              ((sS, ex_s),) if (tau == 0 and h == 0) else (), vV)
                            st[f"b_msk{c % 2}"] = mv
                            dep = (vV, mv)
                        ra = racc[:, it_par * 1024:(it_par + 1) * 1024]
                        if c == 0:
                            st["b_racv"] = P.op("vector", lambda e, o=ra, i=psl: e.tensor_copy(out=o, in_=i),
                                                (dep, (tT, st.get(f"b_rsmm{it_par}", 0))), vV)
                        else:
                            st["b_racv"] = P.op("vector",
                                                lambda e, o=ra, i=psl: e.tensor_tensor(out=o, in0=o, in1=i, op=ALU.add),
                                                (dep, (vV, st["b_racv"])), vV)
                        st[f"b_rac{c % 2}"] = st["b_racv"]
                        prev = (c, dep)
                    emit_av(*prev)
                    ra = racc[:, it_par * 1024:(it_par + 1) * 1024]
                    for half in range(2):
                        P.op("tensor",
                             lambda e, o=prb[0:1, half * 512:(half + 1) * 512], r=ra[:, half * 512:(half + 1) * 512]:
                             e.matmul(o, onesc_sb[:], r, start=True, stop=True),
                             ((vV, st["b_racv"]), (vV, st.get("b_norm", 0))) if half == 0 else (),
                             tT if half == 1 else None)
                    st[f"b_rsmm{it_par}"] = tT.n
                    st[f"b_iter{it_par}"] = tT.n
                    av_t = tT.n
                    rs_t = tT.n
                    l_s = P.op("scalar", lambda e, o=rrow[:], i=prb[0:1, :]: e.activation(o, i, AF.Ln),
                               ((tT, rs_t),), sS)
                    rc_v = P.op("scalar", lambda e, o=rrow[:], i=rrow[:]: e.activation(o, i, AF.Exp, scale=-1.0),
                               ((sS, l_s),), sS)
                    for half in range(2):
                        P.op("tensor",
                             lambda e, o=prb[:, half * 512:(half + 1) * 512], r=rrow[:, half * 512:(half + 1) * 512]:
                             e.matmul(o, onesr_sb[:], r, start=True, stop=True),
                             ((sS, rc_v),) if half == 0 else (), tT if half == 1 else None)
                    bc_t = tT.n
                    st["b_avevac"] = P.op("scalar", lambda e, o=avf[:], i=pav[:]: e.copy(out=o, in_=i),
                                          ((tT, av_t), (vV, st.get("b_norm", 0))), sS)
                    av_s = st["b_avevac"]
                    dsta = attnT[:].rearrange("p (ch q) -> p ch q", ch=NCH)[:, kv * GRP:(kv + 1) * GRP, gp * 256:(gp + 1) * 256]
                    st["b_norm"] = P.op("vector",
                                        lambda e, o=dsta, a_=avf[:].rearrange("p (h q) -> p h q", h=GRP),
                                        b=prb[:].rearrange("p (h q) -> p h q", h=GRP):
                                        e.tensor_tensor(out=o, in0=a_, in1=b, op=ALU.mult),
                                        ((sS, av_s), (tT, bc_t)), vV)
            st["b_attnT_v"] = st["b_norm"]
            if attnT_dbg is not None:
                b_dbg = P.dsem("b_dbg")
                P.dma("scalar", attnT_dbg.rearrange("(ch p) q -> p ch q", p=128),
                      attnT[:].rearrange("p (ch q) -> p ch q", ch=NCH),
                      waits=((vV, st["b_attnT_v"]),), sem=b_dbg)

    if "C1" in PHASES:
        with contextlib.ExitStack() as actx:
            sba = lambda n, s, d: actx.enter_context(nc.sbuf_tensor(n, s, d))
            psa = lambda n, s, d: actx.enter_context(nc.psum_tensor(n, s, d))
            wof = sba("o_wof", [128, 2 * D], F32)
            wob = sba("o_wob", [128, 2 * D], BF16)
            oT = sba("o_oT", [128, TOK], F32)
            xcol = sba("o_xcol", [128, 16 * 128], F32)
            x1c = sba("o_x1c", [128, TOK], F32)
            pO = psa("o_pO", [128, TOK], F32)
            pxT = psa("o_pxT", [128, TOK], F32)
            o_w = [P.dsem("o_w0"), P.dsem("o_w1")]
            o_xc = P.dsem("o_xc")
            o_x1s = P.dsem("o_x1s")

            first_o = True
            for ft in range(NCH):
                wslot = ft % 2
                wsts = wof[:, wslot * D:wslot * D + D]
                wobs = wob[:, wslot * D:wslot * D + D]
                for part in range(4):
                    w0 = ()
                    if part == 0:
                        w0 = guard() if first_o else ((vV, st.get(f"o_wcast{wslot}", 0)),)
                        first_o = False
                    P.dma("sync",
                          wsts.rearrange("p (c f) -> p c f", c=NCH)[:, part * 8:(part + 1) * 8, :],
                          woT[:, ft * 128:(ft + 1) * 128].rearrange("(c p) f -> p c f", p=128)[:, part * 8:(part + 1) * 8, :],
                          waits=w0, sem=o_w[wslot], chain=(part > 0))
                wld = o_w[wslot].n
                st[f"o_wcast{wslot}"] = P.op("vector", lambda e, o=wobs, i=wsts: e.tensor_copy(out=o, in_=i),
                                             ((o_w[wslot], wld), (tT, st.get(f"o_wmm{wslot}", 0))), vV)
                wcast = st[f"o_wcast{wslot}"]
                w0 = ((vV, wcast), (vV, st.get("b_attnT_v", 0)), (sS, st.get("o_oevac", 0)))
                for c in range(NCH):
                    for tq in range(4):
                        P.op("tensor",
                             lambda e, o=pO[:, tq * 512:(tq + 1) * 512], l=wobs[:, c * 128:(c + 1) * 128],
                             r=attnT[:, c * TOK + tq * 512:c * TOK + (tq + 1) * 512],
                             s=(c == 0), t=(c == NCH - 1):
                             e.matmul(o, l, r, start=s, stop=t),
                             w0 if (c == 0 and tq == 0) else (),
                             tT if (c == NCH - 1 and tq == 3) else None)
                st[f"o_wmm{wslot}"] = tT.n
                omm = tT.n
                st["o_oevac"] = P.op("scalar", lambda e, o=oT[:], i=pO[:]: e.copy(out=o, in_=i),
                                     ((tT, omm), (vV, st.get("o_x1tt", 0))), sS)
                oev = st["o_oevac"]
                for part in range(2):
                    P.dma("sync",
                          xcol[:].rearrange("p (j f) -> p j f", f=128)[:, part * 8:(part + 1) * 8, :],
                          x_own[:, ft * 128:(ft + 1) * 128].rearrange("(j p) f -> p j f", p=128)[:, part * 8:(part + 1) * 8, :],
                          waits=((tT, st.get("o_xtr", 0)),) if part == 0 else (), sem=o_xc, chain=(part > 0))
                xld = o_xc.n
                st["o_xtr"] = transposes_128(
                    lambda j: xcol[:, j * 128:(j + 1) * 128],
                    lambda j: pxT[:, j * 128:(j + 1) * 128],
                    16, ((o_xc, xld), (vV, st.get("o_x1tt", 0))), idf_sb)
                xtr = st["o_xtr"]
                st["o_x1tt"] = P.op("vector", lambda e, o=x1c[:], a_=pxT[:], b=oT[:]:
                                    e.tensor_tensor(out=o, in0=a_, in1=b, op=ALU.add),
                                    ((tT, xtr), (sS, oev), (o_x1s, st.get("o_x1sp", 0))), vV)
                st["o_x1sp"] = P.dma("scalar", x1T_d[ft], x1c[:], waits=((vV, st["o_x1tt"]),), sem=o_x1s)
    attnT_scope.close()

    ln2T_scope = contextlib.ExitStack()
    if "C1" in PHASES:
        ln2T = ln2T_scope.enter_context(nc.sbuf_tensor("l_ln2T", [128, NCH * TOK], BF16))
        with contextlib.ExitStack() as actx:
            sba = lambda n, s, d: actx.enter_context(nc.sbuf_tensor(n, s, d))
            xc = sba("l_xc", [128, 2 * TOK], F32)
            dump2 = sba("l_dump", [128, TOK], F32)
            mrow = sba("l_mrow", [1, TOK], F32)
            sqrow = sba("l_sqrow", [1, TOK], F32)
            trow = sba("l_trow", [1, TOK], F32)
            rrow2 = sba("l_rrow2", [1, TOK], F32)
            mrrow = sba("l_mrrow", [1, TOK], F32)
            txc = sba("l_txc", [128, 1024], F32)
            g2row = sba("l_g2row", [1, 1024], BF16)
            l_x = [P.dsem("l_x0"), P.dsem("l_x1")]
            l_g2s = P.dsem("l_g2s")
            with contextlib.ExitStack() as bctx:
                psb = lambda n, s, d: bctx.enter_context(nc.psum_tensor(n, s, d))
                pmean = psb("l_pmean", [1, TOK], F32)
                psq = psb("l_psq", [1, TOK], F32)
                first_l = True
                for c in range(NCH):
                    slot = c % 2
                    xcs = xc[:, slot * TOK:(slot + 1) * TOK]
                    ld = P.dma("sync", xcs, x1T_d[c],
                               waits=(guard() if first_l else ((tT, st.get(f"l_smm{slot}", 0)),)), sem=l_x[slot])
                    first_l = False
                    for tq in range(4):
                        P.op("tensor",
                             lambda e, o=pmean[0:1, tq * 512:(tq + 1) * 512], r=xcs[:, tq * 512:(tq + 1) * 512],
                             s=(c == 0), t=(c == NCH - 1):
                             e.matmul(o, onesc_sb[:], r, start=s, stop=t),
                             ((l_x[slot], ld),) if tq == 0 else (), None)
                    sq_s = P.op("scalar", lambda e, o=dump2[:], i=xcs: e.activation(o, i, AF.Square),
                                ((l_x[slot], ld), (tT, st.get("l_sqmm", 0))), sS)
                    for tq in range(4):
                        P.op("tensor",
                             lambda e, o=psq[0:1, tq * 512:(tq + 1) * 512], r=dump2[:, tq * 512:(tq + 1) * 512],
                             s=(c == 0), t=(c == NCH - 1):
                             e.matmul(o, onesc_sb[:], r, start=s, stop=t),
                             ((sS, sq_s),) if tq == 0 else (), tT if tq == 3 else None)
                    st["l_sqmm"] = tT.n
                    st[f"l_smm{slot}"] = tT.n
                stat_t = tT.n
                mv = P.op("vector", lambda e, o=mrow[:], i=pmean[0:1, :]: e.tensor_scalar(o, i, 1.0 / D, None, ALU.mult),
                          ((tT, stat_t),), vV)
                sv = P.op("vector", lambda e, o=sqrow[:], i=psq[0:1, :]: e.tensor_scalar(o, i, 1.0 / D, None, ALU.mult),
                          ((tT, stat_t),), vV)
                m2v = P.op("vector", lambda e, o=trow[:], a_=mrow[:]: e.tensor_tensor(o, a_, a_, ALU.mult),
                           ((vV, mv),), vV)
                vvv = P.op("vector", lambda e, o=trow[:], a_=sqrow[:], b=trow[:]: e.tensor_tensor(o, a_, b, ALU.subtract),
                           ((vV, m2v), (vV, sv)), vV)
                l_s = P.op("scalar", lambda e, o=trow[:], i=trow[:]: e.activation(o, i, AF.Ln, bias=EPS),
                           ((vV, vvv),), sS)
                r_s = P.op("scalar", lambda e, o=rrow2[:], i=trow[:]: e.activation(o, i, AF.Exp, scale=-0.5),
                           ((sS, l_s),), sS)
                mrv = P.op("vector", lambda e, o=mrrow[:], i=mrow[:], r=rrow2[:]:
                           e.scalar_tensor_tensor(o, i, -1.0, r, ALU.mult, ALU.mult),
                           ((sS, r_s),), vV)
                st["l_rows_v"] = mrv
            with contextlib.ExitStack() as bctx:
                psb = lambda n, s, d: bctx.enter_context(nc.psum_tensor(n, s, d))
                rbc = psb("l_rbc", [128, 1024], F32)
                mrbc = psb("l_mrbc", [128, 1024], F32)
                pg2 = psb("l_pg2", [1, 2 * 1024], F32)
                p2_guard = ((sS, sS.n), (tT, tT.n), (vV, vV.n))
                first_p2 = True
                for th in range(2):
                    for half in range(2):
                        P.op("tensor",
                             lambda e, o=rbc[:, half * 512:(half + 1) * 512],
                             r=rrow2[:, th * 1024 + half * 512:th * 1024 + (half + 1) * 512]:
                             e.matmul(o, onesr_sb[:], r, start=True, stop=True),
                             ((vV, st["l_rows_v"]), (vV, st.get("l_tt1", 0))) if half == 0 else (), None)
                        P.op("tensor",
                             lambda e, o=mrbc[:, half * 512:(half + 1) * 512],
                             r=mrrow[:, th * 1024 + half * 512:th * 1024 + (half + 1) * 512]:
                             e.matmul(o, onesr_sb[:], r, start=True, stop=True),
                             ((vV, st.get("l_tt2", 0)),) if half == 0 else (), tT if half == 1 else None)
                    bc_t = tT.n
                    for c in range(NCH):
                        slot = c % 2
                        xcs = xc[:, slot * TOK:slot * TOK + 1024]
                        ld = P.dma("sync", xcs, x1T_d[c][:, th * 1024:(th + 1) * 1024],
                                   waits=(p2_guard if first_p2 else ((vV, st.get(f"l_n1{slot}", 0)),)),
                                   sem=l_x[slot])
                        first_p2 = False
                        st["l_tt1"] = P.op("vector", lambda e, o=txc[:], a_=xcs, b=rbc[:]:
                                           e.tensor_tensor(out=o, in0=a_, in1=b, op=ALU.mult),
                                           ((l_x[slot], ld), (tT, bc_t), (vV, st.get("l_tt2", 0))), vV)
                        st[f"l_n1{slot}"] = st["l_tt1"]
                        st["l_tt2"] = P.op("vector",
                                           lambda e, o=ln2T[:, c * TOK + th * 1024:c * TOK + (th + 1) * 1024],
                                           a_=txc[:], b=mrbc[:]:
                                           e.tensor_tensor(out=o, in0=a_, in1=b, op=ALU.add),
                                           ((vV, st["l_tt1"]),), vV)
                        g2slot = (c % 2)
                        pgs = pg2[0:1, g2slot * 1024:(g2slot + 1) * 1024]
                        for half in range(2):
                            P.op("tensor",
                                 lambda e, o=pgs[:, half * 512:(half + 1) * 512],
                                 r=ln2T[:, c * TOK + th * 1024 + half * 512:c * TOK + th * 1024 + (half + 1) * 512]:
                                 e.matmul(o, onescb_sb[:], r, start=True, stop=True),
                                 ((vV, st["l_tt2"]), (sS, st.get(f"l_g2e{g2slot}", 0))) if half == 0 else (),
                                 tT if half == 1 else None)
                        gmm = tT.n
                        ge = P.op("scalar", lambda e, o=g2row[:], i=pgs: e.copy(out=o, in_=i),
                                  ((tT, gmm), (l_g2s, st.get("l_g2q", 0))), sS)
                        st[f"l_g2e{g2slot}"] = ge
                        st["l_g2q"] = P.dma("scalar", ln2g_d[c:c + 1, th * 1024:(th + 1) * 1024], g2row[:],
                                            waits=((sS, ge),), sem=l_g2s)

    if "C2" in PHASES:
        with contextlib.ExitStack() as actx:
            sba = lambda n, s, d: actx.enter_context(nc.sbuf_tensor(n, s, d))
            psa = lambda n, s, d: actx.enter_context(nc.psum_tensor(n, s, d))
            wgb = sba("f_wgb", [128, 2 * D], BF16)
            wub = sba("f_wub", [128, 2 * D], BF16)
            zgEO = sba("f_zgEO", [2 * NCH, II // 2], BF16)
            zuEO = sba("f_zuEO", [2 * NCH, II // 2], BF16)
            l2gb = sba("f_l2gb", [2 * NCH, TOK], BF16)
            sg = sba("f_sg", [128, TOK], BF16)
            hb = sba("f_hb", [128, 2 * TOK], BF16)
            hgrow = sba("f_hgrow", [1, TOK], BF16)
            pg = psa("f_pg", [128, TOK], F32)
            pu = psa("f_pu", [128, TOK], F32)

            f_zl = P.dsem("f_zl")
            f_wg = [P.dsem("f_wg0"), P.dsem("f_wg1")]
            f_wu = [P.dsem("f_wu0"), P.dsem("f_wu1")]
            f_hs = P.dsem("f_hs")
            f_hgs = P.dsem("f_hgs")
            P.dma("sync", zgEO[:], zsmg_d, waits=guard(), sem=f_zl)
            P.dma("sync", zuEO[:], zsmu_d, sem=f_zl, chain=True)
            P.dma("sync", l2gb[0:NCH, :], ln2g_d, sem=f_zl, chain=True)
            lg = P.dma("sync", l2gb[NCH:2 * NCH, :], ln2g_d, sem=f_zl, chain=True)
            zs_ld = lg

            for it in range(NIT):
                slot = it % 2
                wgs = wgb[:, slot * D:slot * D + D]
                wus = wub[:, slot * D:slot * D + D]
                lw1 = P.dma("sync", wgs.rearrange("p (c i) -> p c i", i=128), WgB_d[it],
                            waits=((tT, st.get(f"f_gmm{slot}", 0)),), sem=f_wg[slot])
                lw2 = P.dma("sync", wus.rearrange("p (c i) -> p c i", i=128), WuB_d[it],
                            waits=((tT, st.get(f"f_umm{slot}", 0)),), sem=f_wu[slot])
                zg_t = zgEO[(it % 2) * NCH:(it % 2 + 1) * NCH, (it // 2) * 128:(it // 2 + 1) * 128]
                zu_t = zuEO[(it % 2) * NCH:(it % 2 + 1) * NCH, (it // 2) * 128:(it // 2 + 1) * 128]
                w0 = ((f_wg[slot], lw1), (f_zl, zs_ld), (sS, st.get("f_hgevac", 0)),
                      (sS, st.get("f_sil", 0)))
                for c in range(NCH):
                    for tq in range(4):
                        P.op("tensor",
                             lambda e, o=pg[:, tq * 512:(tq + 1) * 512], l=wgs[:, c * 128:(c + 1) * 128],
                             r=ln2T[:, c * TOK + tq * 512:c * TOK + (tq + 1) * 512], s=(c == 0):
                             e.matmul(o, l, r, start=s, stop=False),
                             w0 if (c == 0 and tq == 0) else (), None)
                for tq in range(4):
                    P.op("tensor",
                         lambda e, o=pg[:, tq * 512:(tq + 1) * 512], l=zg_t,
                         r=l2gb[(it % 2) * NCH:(it % 2 + 1) * NCH, tq * 512:(tq + 1) * 512]:
                         e.matmul(o, l, r, start=False, stop=True),
                         (), tT if tq == 3 else None)
                st[f"f_gmm{slot}"] = tT.n
                gmm = tT.n
                sil = P.op("scalar", lambda e, o=sg[:], i=pg[:]: e.activation(o, i, AF.Silu),
                           ((tT, gmm), (vV, st.get("f_htt", 0))), sS)
                st["f_sil"] = sil
                if it > 0:
                    st["f_emit_hg"](it - 1)
                w0 = ((f_wu[slot], lw2), (vV, st.get("f_htt", 0)))
                for c in range(NCH):
                    for tq in range(4):
                        P.op("tensor",
                             lambda e, o=pu[:, tq * 512:(tq + 1) * 512], l=wus[:, c * 128:(c + 1) * 128],
                             r=ln2T[:, c * TOK + tq * 512:c * TOK + (tq + 1) * 512], s=(c == 0):
                             e.matmul(o, l, r, start=s, stop=False),
                             w0 if (c == 0 and tq == 0) else (), None)
                for tq in range(4):
                    P.op("tensor",
                         lambda e, o=pu[:, tq * 512:(tq + 1) * 512], l=zu_t,
                         r=l2gb[(it % 2) * NCH:(it % 2 + 1) * NCH, tq * 512:(tq + 1) * 512]:
                         e.matmul(o, l, r, start=False, stop=True),
                         (), tT if tq == 3 else None)
                st[f"f_umm{slot}"] = tT.n
                umm = tT.n
                hslot = hb[:, (it % 2) * TOK:(it % 2 + 1) * TOK]
                htt = P.op("vector", lambda e, o=hslot, a_=sg[:], b=pu[:]:
                           e.tensor_tensor(out=o, in0=a_, in1=b, op=ALU.mult),
                           ((sS, sil), (tT, umm), (f_hs, st.get(f"f_hsp{it % 2}", 0)),
                            (tT, st.get(f"f_hgm{it % 2}", 0))), vV)
                st["f_htt"] = htt
                st[f"f_htt{it % 2}"] = htt
                st[f"f_hsp{it % 2}"] = P.dma("scalar", hT_d[it], hslot, waits=((vV, htt),), sem=f_hs)

                def emit_hg(jt):
                    hs = hb[:, (jt % 2) * TOK:(jt % 2 + 1) * TOK]
                    for tq in range(4):
                        P.op("tensor",
                             lambda e, o=pg[0:1, tq * 512:(tq + 1) * 512], r=hs[:, tq * 512:(tq + 1) * 512]:
                             e.matmul(o, onescb_sb[:], r, start=True, stop=True),
                             ((vV, st[f"f_htt{jt % 2}"]), (sS, st.get("f_sil", 0)),
                              (sS, st.get("f_hgevac", 0))) if tq == 0 else (),
                             tT if tq == 3 else None)
                    st[f"f_hgm{jt % 2}"] = tT.n
                    hgm = tT.n
                    hge = P.op("scalar", lambda e, o=hgrow[:], i=pg[0:1, :]: e.copy(out=o, in_=i),
                               ((tT, hgm), (f_hgs, st.get("f_hgq", 0))), sS)
                    st["f_hgevac"] = hge
                    st["f_hgq"] = P.dma("scalar", hg_d[jt:jt + 1, :], hgrow[:], waits=((sS, hge),), sem=f_hgs)
                st["f_emit_hg"] = emit_hg
                if it == NIT - 1:
                    emit_hg(it)
    ln2T_scope.close()

    if "C3" in PHASES:
        with contextlib.ExitStack() as actx:
            sba = lambda n, s, d: actx.enter_context(nc.sbuf_tensor(n, s, d))
            psa = lambda n, s, d: actx.enter_context(nc.psum_tensor(n, s, d))
            hq = sba("d_hq", [128, NIT * 512], BF16)
            wdb = sba("d_wdb", [128, 2 * NIT * 128], BF16)
            hgqb = sba("d_hgqb", [NIT, TOK], BF16)
            zsd = sba("d_zsd", [NIT, D], BF16)
            x1q = sba("d_x1q", [128, 2 * 512], F32)
            ob = sba("d_ob", [128, 2 * 512], F32)
            otile = sba("d_otile", [128, 2 * 128], F32)
            pd = psa("d_pd", [128, 2 * 512], F32)
            pout = psa("d_pout", [128, 2 * 512], F32)

            d_zl = P.dsem("d_zl")
            d_hq = P.dsem("d_hq")
            d_w = [P.dsem("d_w0"), P.dsem("d_w1")]
            d_x = [P.dsem("d_x0"), P.dsem("d_x1")]
            d_o = [P.dsem("d_o0"), P.dsem("d_o1")]
            P.dma("sync", zsd[:], zsmd_d, waits=guard(), sem=d_zl)
            l2 = P.dma("sync", hgqb[:], hg_d, sem=d_zl, chain=True)
            zs_ld = l2

            for q in range(4):
                for it in range(NIT):
                    P.dma("sync", hq[:, it * 512:(it + 1) * 512], hT_d[it][:, q * 512:(q + 1) * 512],
                          waits=((tT, st.get("d_lastmm", 0)),) if it == 0 else (),
                          sem=d_hq, chain=(it > 0))
                hq_ld = d_hq.n
                for ft in range(NCH):
                    wslot = ft % 2
                    wds = wdb[:, wslot * NIT * 128:(wslot + 1) * NIT * 128]
                    lw = P.dma("sync", wds.rearrange("p (it i) -> p it i", i=128), WdB_d[ft],
                               waits=((tT, st.get(f"d_wmm{wslot}", 0)),), sem=d_w[wslot])
                    slot = ft % 2
                    pds = pd[:, slot * 512:(slot + 1) * 512]
                    w0 = ((d_w[wslot], lw), (d_zl, zs_ld), (d_hq, hq_ld),
                          (vV, st.get(f"d_ott{slot}", 0)))
                    for it in range(NIT):
                        P.op("tensor",
                             lambda e, o=pds, l=wds[:, it * 128:(it + 1) * 128],
                             r=hq[:, it * 512:(it + 1) * 512], s=(it == 0):
                             e.matmul(o, l, r, start=s, stop=False),
                             w0 if it == 0 else (),
                             None)
                    P.op("tensor",
                         lambda e, o=pds, l=zsd[:, ft * 128:(ft + 1) * 128],
                         r=hgqb[:, q * 512:(q + 1) * 512]:
                         e.matmul(o, l, r, start=False, stop=True),
                         (), tT)
                    st[f"d_wmm{wslot}"] = tT.n
                    st["d_lastmm"] = tT.n
                    dmm = tT.n
                    xld = P.dma("sync", x1q[:, slot * 512:(slot + 1) * 512], x1T_d[ft][:, q * 512:(q + 1) * 512],
                                waits=((vV, st.get(f"d_ott{slot}", 0)),), sem=d_x[slot])
                    ott = P.op("vector",
                               lambda e, o=ob[:, slot * 512:(slot + 1) * 512], a_=pds,
                               b=x1q[:, slot * 512:(slot + 1) * 512]:
                               e.tensor_tensor(out=o, in0=a_, in1=b, op=ALU.add),
                               ((tT, dmm), (d_x[slot], xld), (tT, st.get(f"d_tr{slot}", 0))), vV)
                    st[f"d_ott{slot}"] = ott
                    tr = transposes_128(
                        lambda tau, s=slot: ob[:, s * 512 + tau * 128:s * 512 + (tau + 1) * 128],
                        lambda tau, s=slot: pout[:, s * 512 + tau * 128:s * 512 + (tau + 1) * 128],
                        4, ((vV, ott), (sS, st.get(f"d_oev{slot}", 0))), idf_sb)
                    st[f"d_tr{slot}"] = tr
                    for tau in range(4):
                        oslot = tau % 2
                        oev = P.op("scalar",
                                   lambda e, o=otile[:, oslot * 128:(oslot + 1) * 128],
                                   i=pout[:, slot * 512 + tau * 128:slot * 512 + (tau + 1) * 128]:
                                   e.copy(out=o, in_=i),
                                   ((tT, tr), (d_o[oslot], st.get(f"d_oq{oslot}", 0))), sS)
                        st[f"d_oev{slot}"] = oev
                        st[f"d_oq{oslot}"] = P.dma(
                            "scalar", out_d[(q * 4 + tau) * 128:(q * 4 + tau + 1) * 128, ft * 128:(ft + 1) * 128],
                            otile[:, oslot * 128:(oslot + 1) * 128],
                            waits=((sS, oev),), sem=d_o[oslot])

    P.evts.append(("sync", lambda e: e.nop(), P.dma_waits(), None, 1))

    for s in P.sems:
        s.h = ctx.enter_context(nc.semaphore(s.name))
    with ctx, nc.Block() as block:
        block.sync(lambda e: P.replay("sync", e))
        block.scalar(lambda e: P.replay("scalar", e))
        block.vector(lambda e: P.replay("vector", e))
        block.tensor(lambda e: P.replay("tensor", e))
        block.gpsimd(lambda e: P.replay("gpsimd", e))
    return nc


_NC_CACHE = {}


def make_in_maps(inputs):
    x = np.asarray(inputs["L0_attn_hidden_states"], dtype=np.float32)
    Wq = np.asarray(inputs["L0_attn_Wq"], np.float32)
    Wk = np.asarray(inputs["L0_attn_Wk"], np.float32)
    Wv = np.asarray(inputs["L0_attn_Wv"], np.float32)
    Wo = np.asarray(inputs["L0_attn_Wo"], np.float32)
    qg = np.asarray(inputs["L0_ffn_qw_gate"], np.int32)
    qu = np.asarray(inputs["L0_ffn_qw_up"], np.int32)
    qd = np.asarray(inputs["L0_ffn_qw_down"], np.int32)
    sg = np.asarray(inputs["L0_ffn_s_gate"], np.float32)
    zg = np.asarray(inputs["L0_ffn_z_gate"], np.float32)
    su = np.asarray(inputs["L0_ffn_s_up"], np.float32)
    zu = np.asarray(inputs["L0_ffn_z_up"], np.float32)
    sd = np.asarray(inputs["L0_ffn_s_down"], np.float32)
    zd = np.asarray(inputs["L0_ffn_z_down"], np.float32)

    T = lambda a: np.ascontiguousarray(a.T)
    cos, sin = _rope_tables()
    rot = np.zeros((HD, HD), np.float32)
    rot[:64, 64:] = -np.eye(64)
    rot[64:, :64] = np.eye(64)
    shared = {
        "wqT": T(Wq), "wkT": T(Wk), "wvT": T(Wv), "woT": T(Wo),
        "qgN": qg, "quN": qu, "qdN": qd,
        "sgP": np.ascontiguousarray(sg.reshape(NIT, 128, NCH).transpose(1, 0, 2).reshape(128, NIT * NCH)),
        "suP": np.ascontiguousarray(su.reshape(NIT, 128, NCH).transpose(1, 0, 2).reshape(128, NIT * NCH)),
        "sdP": np.ascontiguousarray(sd.reshape(NCH, 128, NIT).transpose(1, 0, 2).reshape(128, NCH * NIT)),
        "sgT": T(sg), "zgT": T(zg), "suT": T(su), "zuT": T(zu),
        "sdT": T(sd), "zdT": T(zd),
        "cosK": T(cos), "sinK": T(sin),
        "rotT": np.ascontiguousarray(rot.T).astype(BFD),
        "idb": np.eye(128, dtype=BFD), "idf": np.eye(128, dtype=np.float32),
        "onesc": np.ones((128, 1), np.float32), "onescb": np.ones((128, 1), BFD),
        "onesr": np.ones((1, 128), np.float32), "onesrb": np.ones((1, 128), BFD),
        "constv": np.tile(np.array([[EPS, SHIFT, 0, 0, 0, 0, 0, 0]], np.float32), (128, 1)),
    }
    cosKT, sinKT = shared["cosK"], shared["sinK"]
    masks = [_masks_for_parity(0), _masks_for_parity(1)]

    in_maps = []
    for core in range(NCORES):
        b, p = core // 2, core % 2
        xb = x[b]
        m = dict(shared)
        m["x_batch"] = xb
        m["x_own"] = np.ascontiguousarray(xb.reshape(32, 128, D)[p::2].reshape(TOK, D))
        m["cosQ"] = np.ascontiguousarray(cosKT.reshape(HD, 32, 128)[:, p::2].reshape(HD, TOK))
        m["sinQ"] = np.ascontiguousarray(sinKT.reshape(HD, 32, 128)[:, p::2].reshape(HD, TOK))
        m["qmask"] = masks[p]
        in_maps.append(m)
    return in_maps


def kernel(**inputs):
    in_maps = make_in_maps(inputs)
    if "nc" not in _NC_CACHE:
        _NC_CACHE["nc"] = build_nc()
    nc = _NC_CACHE["nc"]
    res = run_bass_kernel_spmd(nc, in_maps, core_ids=list(range(NCORES)))

    out = np.empty((NB, SEQ, D), np.float32)
    for core in range(NCORES):
        b, p = core // 2, core % 2
        oc = res.results[core].get("out")
        if oc is not None:
            out[b].reshape(32, 128, D)[p::2] = oc.reshape(16, 128, D)
    kernel.last_results = res.results
    return out
